# version 05
# baseline (speedup 1.0000x reference)
import numpy as np
import ml_dtypes
from contextlib import ExitStack
import concourse.bass as bass
import concourse.mybir as mybir
from bass_rust import ReduceOp
from concourse.bass_utils import run_bass_kernel_spmd

F32 = mybir.dt.float32
BF16 = mybir.dt.bfloat16
AF = mybir.ActivationFunctionType
OP = mybir.AluOpType
AX = mybir.AxisListType.X
bf16 = ml_dtypes.bfloat16

D = 1024
FF = 4096
NDT = D // 128
NFT = FF // 128
TS = 512
TS3 = 256


def build(T=8192, dump=False):
    NST = T // TS
    NS3 = T // TS3
    NTT = T // 128
    NT3 = TS3 // 128
    nc = bass.Bass("TRN2", target_bir_lowering=False,
                   detect_race_conditions=False)

    x_h = nc.dram_tensor("x", [T, D], F32, kind="ExternalInput").ap()
    y_h = nc.dram_tensor("y", [T, D], F32, kind="ExternalOutput").ap()
    w_h = {}
    for name, shp in [("wk", [D, D]), ("wv", [D, D]), ("wr", [D, D]),
                      ("wo", [D, D]), ("cwr", [D, D]), ("cwk", [D, FF]),
                      ("cwv", [FF, D]), ("w1", [FF, D]), ("w2", [D, FF])]:
        w_h[name] = nc.dram_tensor(name, shp, F32, kind="ExternalInput").ap()
    v_h = {}
    for name in ["ln1g", "ln1b", "ln2g", "ln2b", "mk", "mv", "mr", "td",
                 "tf", "cmk", "cmr", "b2"]:
        v_h[name] = nc.dram_tensor(name, [D], F32, kind="ExternalInput").ap()
    v_h["b1"] = nc.dram_tensor("b1", [FF], F32, kind="ExternalInput").ap()
    idf_h = nc.dram_tensor("idf_h", [128, 128], F32, kind="ExternalInput").ap()
    idb_h = nc.dram_tensor("idb_h", [128, 128], BF16, kind="ExternalInput").ap()
    cst_h = nc.dram_tensor("cst_h", [128, 128], BF16, kind="ExternalInput").ap()
    onr_h = nc.dram_tensor("onr_h", [1, 128], BF16, kind="ExternalInput").ap()
    cc_h = nc.dram_tensor("cc_h", [128, 4], F32, kind="ExternalInput").ap()
    onesf_h = nc.dram_tensor("onesf_h", [128, 128], F32,
                             kind="ExternalInput").ap()
    onesb_h = nc.dram_tensor("onesb_h", [128, 512], BF16,
                             kind="ExternalInput").ap()

    skind = "ExternalOutput" if dump else "Internal"
    xn_s = nc.dram_tensor("xn_s", [NST, NDT, 128, TS], BF16, kind=skind).ap()
    y1_s = nc.dram_tensor("y1_s", [NST, NDT, 128, TS], BF16, kind=skind).ap()
    x2_s = nc.dram_tensor("x2_s", [NST, NDT, 128, TS], BF16, kind=skind).ap()
    rc_s = nc.dram_tensor("rc_s", [NST, NDT, 128, TS], BF16, kind=skind).ap()
    xk2_s = nc.dram_tensor("xk2_s", [NST, NDT, 128, TS], BF16, kind=skind).ap()
    at_s = nc.dram_tensor("at_s", [NTT, 128, D], BF16, kind=skind).ap()

    A = nc.alloc_sbuf_tensor
    idf = A("idf", [128, 128], F32)
    idb = A("idb", [128, 128], BF16)
    cst = A("cst", [128, 128], BF16)
    onr = A("onr", [1, 128], BF16)
    VC = ["ln1g", "ln1b", "ln2g", "ln2b", "mk", "mv", "mr", "cmk", "cmr",
          "dk", "c1", "c2"]
    vc = {n: A("vc_" + n, [128, NDT], F32) for n in VC}
    b1c = A("b1c", [128, NFT], F32)
    b2r = A("b2r", [1, D], BF16)
    cc = A("cc", [128, 4], F32)
    onesb = A("onesb", [128, 512], BF16)
    onecol = cc[:, 0:1]
    eps3 = cc[:, 1:2]
    eps5 = cc[:, 2:3]
    zcol = cc[:, 3:4]
    SEM = nc.alloc_semaphore

    p_c, p_v = SEM("p_c"), SEM("p_v")
    p_ts, p_vs, p_sc = SEM("p_ts"), SEM("p_vs"), SEM("p_sc")
    vec_list = ["ln1g", "ln1b", "ln2g", "ln2b", "mk", "mv", "mr",
                "cmk", "cmr", "td", "tf"]
    NV = len(vec_list)
    with ExitStack() as es:
        vstage = es.enter_context(nc.sbuf_tensor("vstage", [32, 128], F32))[:]
        b2stage = es.enter_context(nc.sbuf_tensor("b2stage", [1, D], F32))[:]
        pvc = es.enter_context(nc.psum_tensor("pvc", [128, 32], F32))[:]
        with nc.Block() as blk:
            @blk.sync
            def _(s):
                s.dma_start(idf[:], idf_h).then_inc(p_c, 16)
                s.dma_start(idb[:], idb_h).then_inc(p_c, 16)
                s.dma_start(cst[:], cst_h).then_inc(p_c, 16)
                s.dma_start(onr[:], onr_h).then_inc(p_c, 16)
                s.dma_start(b2stage[:], v_h["b2"].rearrange("(a d) -> a d", a=1)
                            ).then_inc(p_c, 16)
                s.dma_start(cc[:], cc_h).then_inc(p_c, 16)
                s.dma_start(onesb[:], onesb_h).then_inc(p_c, 16)
                for i, n in enumerate(vec_list):
                    if i >= 1:
                        s.wait_ge(p_ts, i)
                    s.dma_start(vstage[0:NDT, :],
                                v_h[n].rearrange("(a b) -> a b", b=128)
                                ).then_inc(p_v, 16)
                s.wait_ge(p_ts, NV)
                s.dma_start(vstage[:, :],
                            v_h["b1"].rearrange("(a b) -> a b", b=128)
                            ).then_inc(p_v, 16)

            @blk.tensor
            def _(t):
                t.wait_ge(p_c, 16 * 7)
                for i in range(NV):
                    t.wait_ge(p_v, 16 * (i + 1))
                    if i >= 1:
                        t.wait_ge(p_vs, i)
                    t.transpose(pvc[:, 0:NDT], vstage[0:NDT, :],
                                idf[0:NDT, 0:NDT]).then_inc(p_ts, 1)
                t.wait_ge(p_v, 16 * (NV + 1))
                t.wait_ge(p_vs, NV)
                t.transpose(pvc[:, 0:32], vstage[:, :], idf[0:32, 0:32]
                            ).then_inc(p_ts, 1)

            @blk.vector
            def _(v):
                for i, n in enumerate(vec_list):
                    v.wait_ge(p_ts, i + 1)
                    dst = {"td": vc["dk"], "tf": vc["c2"]}.get(n, vc.get(n))
                    v.tensor_copy(dst[:, 0:NDT], pvc[:, 0:NDT]).then_inc(p_vs, 1)
                v.wait_ge(p_ts, NV + 1)
                v.tensor_copy(b1c[:], pvc[:, 0:32]).then_inc(p_vs, 1)
                v.wait_ge(p_c, 16 * 7)
                v.tensor_copy(b2r[:], b2stage[:])
                v.wait_ge(p_sc, 1)
                v.tensor_tensor(vc["c2"][:], vc["c2"][:], vc["c1"][:],
                                OP.subtract)

            @blk.scalar
            def _(s):
                s.wait_ge(p_vs, NV)
                s.activation(vc["c1"][:], vc["dk"][:], AF.Exp)
                s.activation(vc["dk"][:], vc["c1"][:], AF.Exp, scale=-1.0)
                s.activation(vc["c1"][:], vc["c1"][:], AF.Exp)
                s.activation(vc["c2"][:], vc["c2"][:], AF.Exp
                             ).then_inc(p_sc, 1)

    G0 = 4
    NG0 = NTT // G0
    s0 = {n: SEM("s0" + n) for n in ["bn", "vr", "iv", "xn", "tp", "ev"]}
    s0ld = [SEM("s0lda"), SEM("s0ldb")]
    s0st = [SEM("s0sta"), SEM("s0stb")]
    with ExitStack() as es:
        S = lambda nm, sh, dt=F32: es.enter_context(nc.sbuf_tensor(nm, sh, dt))[:]
        PS = lambda nm, sh, dt=F32: es.enter_context(nc.psum_tensor(nm, sh, dt))[:]
        xst = [S(f"xst{i}", [128, D]) for i in range(2 * G0)]
        bns = S("bns", [128, 12])
        st0 = S("st0", [128, 4 * G0])
        iv0 = S("iv0", [128, 2 * G0])
        xnb0 = [S(f"xnb0{i}", [128, D], BF16) for i in range(2)]
        xcm = [[S(f"xcm{p}_{d}", [128, TS], BF16) for d in range(NDT)]
               for p in range(2)]
        ptp0 = [[PS(f"ptp0{p}_{h}", [128, 512], BF16) for h in range(2)]
                for p in range(2)]
        with nc.Block() as blk:
            @blk.sync
            def _(s):
                for tt in range(NTT):
                    g = tt // G0
                    if tt >= 2 * G0:
                        s.wait_ge(s0["xn"], tt - 2 * G0 + 1)
                    s.dma_start(xst[tt % (2 * G0)],
                                x_h[tt * 128:(tt + 1) * 128, :]
                                ).then_inc(s0ld[g % 2], 16)

            def evac_group(s, g):
                for j in range(G0):
                    tt = g * G0 + j
                    s.wait_ge(s0["tp"], tt + 1)
                    if j == 0 and g >= 2:
                        s.wait_ge(s0st[g % 2], 16 * NDT * ((g - 2) // 2 + 1))
                    for ds in range(NDT):
                        ins = s.copy(xcm[g % 2][ds][:, j * 128:(j + 1) * 128],
                                     ptp0[tt % 2][ds // 4]
                                     [:, (ds % 4) * 128:(ds % 4 + 1) * 128])
                    ins.then_inc(s0["ev"], 1)

            @blk.scalar
            def _(s):
                for g in range(NG0):
                    s.wait_ge(s0["vr"], g + 1)
                    oi = (g % 2) * G0
                    s.activation(iv0[:, oi:oi + G0], iv0[:, oi:oi + G0],
                                 AF.Ln, bias=eps5)
                    s.activation(iv0[:, oi:oi + G0], iv0[:, oi:oi + G0],
                                 AF.Exp, scale=-0.5).then_inc(s0["iv"], 1)
                    if g >= 1:
                        evac_group(s, g - 1)
                evac_group(s, NG0 - 1)

            @blk.vector
            def _(v):
                for g in range(NG0):
                    o = (g % 2) * 2 * G0
                    oi = (g % 2) * G0
                    v.wait_ge(s0ld[g % 2], 16 * G0 * (g // 2 + 1))
                    if g >= 2:
                        v.wait_ge(s0["iv"], g - 1)
                    for j in range(G0):
                        tt = g * G0 + j
                        v.bn_stats(bns[:, 0:6], xst[tt % (2 * G0)][:, 0:512]
                                   ).then_inc(s0["bn"], 1)
                        v.bn_stats(bns[:, 6:12], xst[tt % (2 * G0)][:, 512:D]
                                   ).then_inc(s0["bn"], 1)
                        v.wait_ge(s0["bn"], 3 * tt + 2)
                        v.bn_aggr(st0[:, o + 2 * j:o + 2 * j + 2], bns[:]
                                  ).then_inc(s0["bn"], 1)
                    v.wait_ge(s0["bn"], 3 * G0 * (g + 1))
                    v.tensor_scalar(
                        iv0[:, oi:oi + G0],
                        bass.AP(st0.tensor, st0.offset + o + 1,
                                [[st0.ap[0][0], 128], [2, G0]]),
                        1.0, None, OP.mult).then_inc(s0["vr"], 1)
                    v.wait_ge(s0["iv"], g + 1)
                    for j in range(G0):
                        tt = g * G0 + j
                        if tt >= 2:
                            v.wait_ge(s0["tp"], tt - 1)
                        v.tensor_scalar(xnb0[tt % 2], xst[tt % (2 * G0)],
                                        st0[:, o + 2 * j:o + 2 * j + 1],
                                        iv0[:, oi + j:oi + j + 1],
                                        OP.subtract, OP.mult
                                        ).then_inc(s0["xn"], 1)

            @blk.tensor
            def _(t):
                for tt in range(NTT):
                    t.wait_ge(s0["xn"], tt + 1)
                    if tt >= 2:
                        t.wait_ge(s0["ev"], tt - 1)
                    for ds in range(NDT):
                        ins = t.transpose(
                            ptp0[tt % 2][ds // 4]
                            [:, (ds % 4) * 128:(ds % 4 + 1) * 128],
                            xnb0[tt % 2][:, ds * 128:(ds + 1) * 128], idb[:])
                    ins.then_inc(s0["tp"], 1)

            @blk.gpsimd
            def _(gp):
                for g in range(NG0):
                    gp.wait_ge(s0["ev"], G0 * (g + 1))
                    for ds in range(NDT):
                        gp.dma_start(xn_s[g, ds], xcm[g % 2][ds]
                                     ).then_inc(s0st[g % 2], 16)
                gp.wait_ge(s0st[0], 16 * NDT * ((NG0 + 1) // 2))
                if NG0 > 1:
                    gp.wait_ge(s0st[1], 16 * NDT * (NG0 // 2))

    s1 = {n: SEM("s1" + n) for n in
          ["cv", "mmS", "mmK", "vq", "vv", "vb", "vd",
           "si", "sk", "sr", "sd", "vsc", "vps",
           "g1m", "g2m", "vmx", "vmo", "shb", "sc2", "gy"]}
    s1ld = [SEM("s1lda"), SEM("s1ldb")]
    s1wl = [SEM("s1wla"), SEM("s1wlb")]
    s1st = [SEM("s1sta"), SEM("s1stb")]
    with ExitStack() as es:
        S = lambda nm, sh, dt=F32: es.enter_context(nc.sbuf_tensor(nm, sh, dt))[:]
        PS = lambda nm, sh, dt=F32: es.enter_context(nc.psum_tensor(nm, sh, dt))[:]
        wstg = [S(f"wstg{i}", [128, D]) for i in range(2)]
        wkb = [S(f"wkb{i}", [128, D], BF16) for i in range(NDT)]
        wvb = [S(f"wvb{i}", [128, D], BF16) for i in range(NDT)]
        wrb = [S(f"wrb{i}", [128, D], BF16) for i in range(NDT)]
        xn1 = [[S(f"xn1{p}_{d}", [128, TS], BF16) for d in range(NDT)]
               for p in range(2)]
        xsq1 = [S(f"xsq1{d}", [128, TS], BF16) for d in range(NDT)]
        var1 = [S(f"var1{p}", [128, TS]) for p in range(2)]
        musb1 = S("musb1", [128, TS])
        inv1 = [S(f"inv1{p}", [128, TS], BF16) for p in range(2)]
        t0fg = [S(f"t0fg{d}", [128, TS], BF16) for d in range(NDT)]
        t0fm = [S(f"t0fm{p}", [128, TS], BF16) for p in range(2)]
        hbuf = [S(f"hbuf{d}", [128, TS + 1], BF16) for d in range(NDT)]
        dmix = [S(f"dmix{p}", [128, TS], BF16) for p in range(2)]
        xkb = [S(f"xkb{d}", [128, TS], BF16) for d in range(NDT)]
        xvb = [S(f"xvb{d}", [128, TS], BF16) for d in range(NDT)]
        xrb = [S(f"xrb{d}", [128, TS], BF16) for d in range(NDT)]
        drep = [S(f"drep{d}", [128, TS], BF16) for d in range(NDT)]
        ekb = [S(f"ekb{p}", [128, TS], BF16) for p in range(2)]
        ekvb = [S(f"ekvb{p}", [128, TS], BF16) for p in range(2)]
        sab = [S(f"sab{d}", [128, TS], BF16) for d in range(NDT)]
        sbb = [S(f"sbb{d}", [128, TS], BF16) for d in range(NDT)]
        ekc2 = [S(f"ekc2{p}", [128, TS], BF16) for p in range(2)]
        ekvc2 = [S(f"ekvc2{p}", [128, TS], BF16) for p in range(2)]
        numb = [S(f"numb{p}", [128, TS], BF16) for p in range(2)]
        denb = [S(f"denb{p}", [128, TS], BF16) for p in range(2)]
        sg1 = [S(f"sg1{p}", [128, TS], BF16) for p in range(2)]
        sg2 = [S(f"sg2{p}", [128, TS], BF16) for p in range(2)]
        rden = [S(f"rden{p}", [128, TS], BF16) for p in range(2)]
        ybuf = [[S(f"ybuf{p}_{d}", [128, TS], BF16) for d in range(NDT)]
                for p in range(2)]
        pmu1 = PS("pmu1", [128, TS])
        pe21 = PS("pe21", [128, TS])
        pk1 = [PS(f"pk1{p}", [128, TS]) for p in range(2)]
        pv1 = [PS(f"pv1{p}", [128, TS]) for p in range(2)]
        pr1 = [PS(f"pr1{p}", [128, TS]) for p in range(2)]

        def _y1gp(gp, st, ht):
            hh = st * NDT + ht
            p = ht % 2
            gp.wait_ge(s1["vd"], hh + 1)
            gp.wait_ge(s1["sd"], hh + 1)
            gp.wait_ge(s1["sr"], 3 * hh + 3)
            if st >= 2 and ht == 0:
                gp.wait_ge(s1st[st % 2], 16 * NDT * ((st - 2) // 2 + 1))
            gp.tensor_tensor(numb[p], numb[p], rden[p], OP.mult)
            gp.tensor_tensor(ybuf[st % 2][ht], numb[p], sg1[p], OP.mult
                             ).then_inc(s1["gy"], 1)
            gp.wait_ge(s1["gy"], hh + 1)
            gp.dma_start(y1_s[st, ht], ybuf[st % 2][ht]
                         ).then_inc(s1st[st % 2], 16)

        def _rden(s, st, ht):
            hh = st * NDT + ht
            p = ht % 2
            s.wait_ge(s1["vd"], hh + 1)
            s.activation(rden[p], denb[p], AF.Ln)
            s.activation(rden[p], rden[p], AF.Exp, scale=-1.0
                         ).then_inc(s1["sd"], 1)

        with nc.Block() as blk:
            @blk.sync
            def _(s):
                for i in range(3 * NDT):
                    if i >= 2:
                        s.wait_ge(s1["cv"], i - 1)
                    nme, dt = ["wk", "wv", "wr"][i // NDT], i % NDT
                    s.dma_start(wstg[i % 2], w_h[nme][dt * 128:(dt + 1) * 128, :]
                                ).then_inc(s1wl[i % 2], 16)
                for st in range(NST):
                    if st >= 2:
                        s.wait_ge(s1["g1m"], NDT * (st - 1))
                        s.wait_ge(s1["vps"], st - 1)
                    for dt in range(NDT):
                        s.dma_start(xn1[st % 2][dt], xn_s[st, dt]
                                    ).then_inc(s1ld[st % 2], 16)

            @blk.vector
            def _(v):
                for i in range(3 * NDT):
                    v.wait_ge(s1wl[i % 2], 16 * (i // 2 + 1))
                    dst = [wkb, wvb, wrb][i // NDT][i % NDT]
                    v.tensor_copy(dst, wstg[i % 2]).then_inc(s1["cv"], 1)
                for dt in range(NDT):
                    v.tensor_scalar(drep[dt], onesb[:, 0:TS],
                                    vc["dk"][:, dt:dt + 1], None, OP.mult)
                    v.tensor_copy(hbuf[dt][:, 0:1], zcol)
                    v.tensor_copy(sab[dt][:, TS - 1:TS], zcol)
                    v.tensor_copy(sbb[dt][:, TS - 1:TS], zcol)
                for st in range(NST):
                    v.wait_ge(s1ld[st % 2], 16 * NDT * (st // 2 + 1))
                    for dt in range(NDT):
                        if dt == 0 and st >= 1:
                            v.wait_ge(s1["mmS"], 2 * st)
                        v.tensor_tensor(xsq1[dt], xn1[st % 2][dt],
                                        xn1[st % 2][dt], OP.mult
                                        ).then_inc(s1["vq"], 1)
                    v.wait_ge(s1["mmS"], 2 * (st + 1))
                    if st >= 2:
                        v.wait_ge(s1["si"], st - 1)
                    if st >= 1:
                        v.wait_ge(s1["g1m"], NDT * st)
                    v.tensor_copy(musb1, pmu1).then_inc(s1["vps"], 1)
                    v.wait_ge(s1["vps"], st + 1)
                    v.tensor_tensor(var1[st % 2], musb1, musb1, OP.mult)
                    v.scalar_tensor_tensor(var1[st % 2], pe21, 1.0, var1[st % 2],
                                           op0=OP.mult, op1=OP.subtract
                                           ).then_inc(s1["vv"], 1)
                    v.wait_ge(s1["si"], st + 1)
                    for dt in range(NDT):
                        if st * NDT + dt >= 2:
                            v.wait_ge(s1["shb"], st * NDT + dt - 1)
                        if st > 0:
                            v.tensor_copy(hbuf[dt][:, 0:1],
                                          hbuf[dt][:, TS:TS + 1])
                        v.wait_ge(s1["g1m"], st * NDT + dt + 1)
                        v.tensor_tensor(t0fm[dt % 2], t0fg[dt],
                                        inv1[st % 2], OP.mult
                                        ).then_inc(s1["vmx"], 1)
                    if st >= 1:
                        v.wait_ge(s1["mmK"], 24 * st)
                    for dt in range(NDT):
                        v.wait_ge(s1["g2m"], st * NDT + dt + 1)
                        h1 = hbuf[dt][:, 0:TS]
                        v.scalar_tensor_tensor(xkb[dt], dmix[dt % 2],
                                               vc["mk"][:, dt:dt + 1], h1,
                                               op0=OP.mult, op1=OP.add)
                        v.scalar_tensor_tensor(xvb[dt], dmix[dt % 2],
                                               vc["mv"][:, dt:dt + 1], h1,
                                               op0=OP.mult, op1=OP.add)
                        ins = v.scalar_tensor_tensor(xrb[dt], dmix[dt % 2],
                                                     vc["mr"][:, dt:dt + 1], h1,
                                                     op0=OP.mult, op1=OP.add)
                        ins.then_inc(s1["vmo"], 1)
                    for ht in range(NDT):
                        hh = st * NDT + ht
                        p = ht % 2
                        v.wait_ge(s1["sk"], hh + 1)
                        v.wait_ge(s1["mmK"], 3 * hh + 3)
                        if hh >= 2:
                            v.wait_ge(s1["sc2"], hh - 1)
                        v.tensor_tensor(ekvb[p], ekb[p], pv1[p], OP.mult
                                        ).then_inc(s1["vb"], 1)
                        v.tensor_tensor_scan(sbb[ht], drep[ht], ekb[p],
                                             sbb[ht][:, TS - 1:TS],
                                             OP.mult, OP.add
                                             ).then_inc(s1["vsc"], 1)
                        v.tensor_tensor_scan(sab[ht], drep[ht], ekvb[p],
                                             sab[ht][:, TS - 1:TS],
                                             OP.mult, OP.add
                                             ).then_inc(s1["vsc"], 1)
                        v.wait_ge(s1["vsc"], 2 * (hh + 1))
                        v.wait_ge(s1["sc2"], hh + 1)
                        if hh >= 2:
                            v.wait_ge(s1["gy"], hh - 1)
                            v.wait_ge(s1["sd"], hh - 1)
                        v.scalar_tensor_tensor(numb[p], sab[ht],
                                               vc["c1"][:, ht:ht + 1],
                                               ekvc2[p], op0=OP.mult,
                                               op1=OP.add)
                        v.scalar_tensor_tensor(denb[p], sbb[ht],
                                               vc["c1"][:, ht:ht + 1],
                                               ekc2[p], op0=OP.mult,
                                               op1=OP.add).then_inc(s1["vd"], 1)

            @blk.scalar
            def _(s):
                for st in range(NST):
                    s.wait_ge(s1["vv"], st + 1)
                    if st >= 2:
                        s.wait_ge(s1["vmx"], NDT * (st - 1))
                    s.activation(var1[st % 2], var1[st % 2], AF.Ln,
                                 bias=eps3)
                    s.activation(inv1[st % 2], var1[st % 2], AF.Exp, scale=-0.5
                                 ).then_inc(s1["si"], 1)
                    for dt in range(NDT):
                        hh8 = st * NDT + dt
                        s.wait_ge(s1["vmx"], hh8 + 1)
                        if st >= 1:
                            s.wait_ge(s1["g2m"], hh8 - NDT + 1)
                            s.wait_ge(s1["vmo"], hh8 - NDT + 1)
                        s.activation(hbuf[dt][:, 1:TS + 1], t0fm[dt % 2],
                                     AF.Identity,
                                     scale=vc["ln1g"][:, dt:dt + 1],
                                     bias=vc["ln1b"][:, dt:dt + 1]
                                     ).then_inc(s1["shb"], 1)
                    for ht in range(NDT):
                        hh = st * NDT + ht
                        p = ht % 2
                        s.wait_ge(s1["mmK"], 3 * hh + 3)
                        if hh >= 2:
                            s.wait_ge(s1["vd"], hh - 1)
                            s.wait_ge(s1["gy"], hh - 1)
                        s.activation(ekb[p], pk1[p], AF.Exp).then_inc(s1["sk"], 1)
                        s.activation(ekc2[p], ekb[p], AF.Copy,
                                     scale=vc["c2"][:, ht:ht + 1])
                        s.activation(sg1[p], pr1[p], AF.Exp, scale=-1.0
                                     ).then_inc(s1["sr"], 1)
                        s.activation(sg2[p], sg1[p], AF.Ln, bias=onecol
                                     ).then_inc(s1["sr"], 1)
                        s.activation(sg1[p], sg2[p], AF.Exp, scale=-1.0
                                     ).then_inc(s1["sr"], 1)
                        s.wait_ge(s1["vb"], hh + 1)
                        s.activation(ekvc2[p], ekvb[p], AF.Copy,
                                     scale=vc["c2"][:, ht:ht + 1]
                                     ).then_inc(s1["sc2"], 1)
                        if ht > 0:
                            _rden(s, st, ht - 1)
                    _rden(s, st, NDT - 1)

            @blk.tensor
            def _(t):
                t.wait_ge(s1["cv"], 3 * NDT)
                for st in range(NST):
                    t.wait_ge(s1ld[st % 2], 16 * NDT * (st // 2 + 1))
                    if st > 0:
                        t.wait_ge(s1["vv"], st)
                    for dt in range(NDT):
                        ins = t.matmul(pmu1[:], cst[:], xn1[st % 2][dt],
                                       start=(dt == 0), stop=(dt == NDT - 1))
                    ins.then_inc(s1["mmS"], 1)
                    for dt in range(NDT):
                        t.wait_ge(s1["vq"], st * NDT + dt + 1)
                        ins = t.matmul(pe21[:], cst[:], xsq1[dt],
                                       start=(dt == 0), stop=(dt == NDT - 1))
                    ins.then_inc(s1["mmS"], 1)
                    t.wait_ge(s1["vmo"], NDT * (st + 1))
                    for ht in range(NDT):
                        hh = st * NDT + ht
                        p = ht % 2
                        if hh >= 2:
                            t.wait_ge(s1["sk"], hh - 1)
                            t.wait_ge(s1["vb"], hh - 1)
                            t.wait_ge(s1["sr"], 3 * (hh - 2) + 1)
                        for dt in range(NDT):
                            t.matmul(pk1[p], wkb[dt][:, ht * 128:(ht + 1) * 128],
                                     xkb[dt], start=(dt == 0),
                                     stop=(dt == NDT - 1))
                        for dt in range(NDT):
                            t.matmul(pv1[p], wvb[dt][:, ht * 128:(ht + 1) * 128],
                                     xvb[dt], start=(dt == 0),
                                     stop=(dt == NDT - 1))
                        for dt in range(NDT):
                            ins = t.matmul(pr1[p],
                                           wrb[dt][:, ht * 128:(ht + 1) * 128],
                                           xrb[dt], start=(dt == 0),
                                           stop=(dt == NDT - 1))
                        ins.then_inc(s1["mmK"], 3)

            @blk.gpsimd
            def _(gp):
                for st in range(NST):
                    gp.wait_ge(s1["vps"], st + 1)
                    if st >= 1:
                        gp.wait_ge(s1["vmx"], NDT * st)
                    for dt in range(NDT):
                        gp.tensor_tensor(t0fg[dt], xn1[st % 2][dt], musb1,
                                         OP.subtract).then_inc(s1["g1m"], 1)
                    for dt in range(NDT):
                        gp.wait_ge(s1["shb"], st * NDT + dt + 1)
                        if st * NDT + dt >= 2:
                            gp.wait_ge(s1["vmo"], st * NDT + dt - 1)
                        gp.tensor_tensor(dmix[dt % 2], hbuf[dt][:, 1:TS + 1],
                                         hbuf[dt][:, 0:TS], OP.subtract
                                         ).then_inc(s1["g2m"], 1)
                    for ht in range(NDT):
                        _y1gp(gp, st, ht)
                gp.wait_ge(s1st[0], 16 * NDT * ((NST + 1) // 2))
                if NST > 1:
                    gp.wait_ge(s1st[1], 16 * NDT * (NST // 2))

    s2 = {n: SEM("s2" + n) for n in
          ["cv", "mmS", "mmD", "mmR", "vx", "vq", "vv", "vps", "si", "sg",
           "g1m", "g2m", "vmx", "vmo", "shb"]}
    s2ld = [SEM("s2lda"), SEM("s2ldb")]
    s2wl = [SEM("s2wla"), SEM("s2wlb")]
    sstx = [SEM("sstxa"), SEM("sstxb")]
    sstk = [SEM("sstka"), SEM("sstkb")]
    with ExitStack() as es:
        S = lambda nm, sh, dt=F32: es.enter_context(nc.sbuf_tensor(nm, sh, dt))[:]
        PS = lambda nm, sh, dt=F32: es.enter_context(nc.psum_tensor(nm, sh, dt))[:]
        wstg = [S(f"w2stg{i}", [128, D]) for i in range(2)]
        wob = [S(f"wob{i}", [128, D], BF16) for i in range(NDT)]
        cwrb = [S(f"cwrb{i}", [128, D], BF16) for i in range(NDT)]
        xn2 = [[S(f"xn2{p}_{d}", [128, TS], BF16) for d in range(NDT)]
               for p in range(2)]
        yy2 = [[S(f"yy2{p}_{d}", [128, TS], BF16) for d in range(NDT)]
               for p in range(2)]
        x2b = [[S(f"x2b{p}_{d}", [128, TS], BF16) for d in range(NDT)]
               for p in range(2)]
        xsq2 = [S(f"xsq2{d}", [128, TS], BF16) for d in range(NDT)]
        var2 = [S(f"var2{p}", [128, TS]) for p in range(2)]
        musb2 = [S(f"musb2{p}", [128, TS]) for p in range(2)]
        inv2 = [S(f"inv2{p}", [128, TS], BF16) for p in range(2)]
        t2g = [S(f"t2g{d}", [128, TS], BF16) for d in range(NDT)]
        t2m = [S(f"t2m{p}", [128, TS], BF16) for p in range(2)]
        h2buf = [S(f"h2buf{d}", [128, TS + 1], BF16) for d in range(NDT)]
        dmix2 = [S(f"dmix2{p}", [128, TS], BF16) for p in range(2)]
        xk2b = [[S(f"xk2b{p}_{d}", [128, TS], BF16) for d in range(NDT)]
                for p in range(2)]
        xr2b = [S(f"xr2b{d}", [128, TS], BF16) for d in range(NDT)]
        sg21 = [S(f"sg21{p}", [128, TS], BF16) for p in range(2)]
        sg22 = [S(f"sg22{p}", [128, TS], BF16) for p in range(2)]
        rcb = [[S(f"rcb{p}_{d}", [128, TS], BF16) for d in range(NDT)]
               for p in range(2)]
        pmu2 = PS("pmu2", [128, TS])
        pe22 = PS("pe22", [128, TS])
        pd2 = [PS(f"pd2{p}", [128, TS]) for p in range(2)]
        pr2 = [PS(f"pr2{p}", [128, TS]) for p in range(2)]
        with nc.Block() as blk:
            @blk.sync
            def _(s):
                for i in range(2 * NDT):
                    if i >= 2:
                        s.wait_ge(s2["cv"], i - 1)
                    nme, dt = ["wo", "cwr"][i // NDT], i % NDT
                    s.dma_start(wstg[i % 2], w_h[nme][dt * 128:(dt + 1) * 128, :]
                                ).then_inc(s2wl[i % 2], 16)
                for st in range(NST):
                    if st >= 2:
                        s.wait_ge(s2["vx"], NDT * (st - 1))
                        s.wait_ge(s2["mmD"], NDT * (st - 1))
                    s.wait_ge(s1st[st % 2], 16 * NDT * (st // 2 + 1))
                    for dt in range(NDT):
                        s.dma_start(xn2[st % 2][dt], xn_s[st, dt]
                                    ).then_inc(s2ld[st % 2], 16)
                    for dt in range(NDT):
                        s.dma_start(yy2[st % 2][dt], y1_s[st, dt]
                                    ).then_inc(s2ld[st % 2], 16)

            @blk.vector
            def _(v):
                for i in range(2 * NDT):
                    v.wait_ge(s2wl[i % 2], 16 * (i // 2 + 1))
                    dst = [wob, cwrb][i // NDT][i % NDT]
                    v.tensor_copy(dst, wstg[i % 2]).then_inc(s2["cv"], 1)
                for dt in range(NDT):
                    v.tensor_copy(h2buf[dt][:, 0:1], zcol)
                for st in range(NST + 1):
                    if st < NST:
                        for et in range(NDT):
                            ee = st * NDT + et
                            v.wait_ge(s2["mmD"], ee + 1)
                            if st >= 2 and et == 0:
                                v.wait_ge(sstx[st % 2],
                                          16 * NDT * ((st - 2) // 2 + 1))
                                v.wait_ge(s2["g1m"], NDT * (st - 1))
                            v.tensor_tensor(x2b[st % 2][et], xn2[st % 2][et],
                                            pd2[et % 2], OP.add
                                            ).then_inc(s2["vx"], 1)
                        for dt in range(NDT):
                            if dt == 0 and st >= 1:
                                v.wait_ge(s2["mmS"], 2 * st)
                            v.tensor_tensor(xsq2[dt], x2b[st % 2][dt],
                                            x2b[st % 2][dt], OP.mult
                                            ).then_inc(s2["vq"], 1)
                        v.wait_ge(s2["mmS"], 2 * (st + 1))
                        if st >= 2:
                            v.wait_ge(s2["si"], st - 1)
                            v.wait_ge(s2["g1m"], NDT * (st - 1))
                        v.tensor_copy(musb2[st % 2], pmu2
                                      ).then_inc(s2["vps"], 1)
                        v.wait_ge(s2["vps"], st + 1)
                        v.tensor_tensor(var2[st % 2], musb2[st % 2],
                                        musb2[st % 2], OP.mult)
                        v.scalar_tensor_tensor(var2[st % 2], pe22, 1.0,
                                               var2[st % 2], op0=OP.mult,
                                               op1=OP.subtract
                                               ).then_inc(s2["vv"], 1)
                    if st >= 1:
                        sb = st - 1
                        v.wait_ge(s2["si"], sb + 1)
                        for dt in range(NDT):
                            g = sb * NDT + dt
                            if g >= 2:
                                v.wait_ge(s2["shb"], g - 1)
                            if sb > 0:
                                v.tensor_copy(h2buf[dt][:, 0:1],
                                              h2buf[dt][:, TS:TS + 1])
                            v.wait_ge(s2["g1m"], g + 1)
                            v.tensor_tensor(t2m[dt % 2], t2g[dt],
                                            inv2[sb % 2], OP.mult
                                            ).then_inc(s2["vmx"], 1)
                        if sb >= 1:
                            v.wait_ge(s2["mmR"], NDT * sb)
                        for dt in range(NDT):
                            g = sb * NDT + dt
                            v.wait_ge(s2["g2m"], g + 1)
                            if sb >= 2 and dt == 0:
                                v.wait_ge(sstk[sb % 2],
                                          16 * 2 * NDT * ((sb - 2) // 2 + 1))
                            h1 = h2buf[dt][:, 0:TS]
                            v.scalar_tensor_tensor(xk2b[sb % 2][dt],
                                                   dmix2[dt % 2],
                                                   vc["cmk"][:, dt:dt + 1], h1,
                                                   op0=OP.mult, op1=OP.add)
                            ins = v.scalar_tensor_tensor(xr2b[dt],
                                                         dmix2[dt % 2],
                                                         vc["cmr"][:, dt:dt + 1],
                                                         h1, op0=OP.mult,
                                                         op1=OP.add)
                            ins.then_inc(s2["vmo"], 1)

            @blk.scalar
            def _(s):
                for st in range(NST + 1):
                    if st < NST:
                        s.wait_ge(s2["vv"], st + 1)
                        if st >= 2:
                            s.wait_ge(s2["vmx"], NDT * (st - 1))
                        s.activation(var2[st % 2], var2[st % 2], AF.Ln,
                                     bias=eps3)
                        s.activation(inv2[st % 2], var2[st % 2], AF.Exp,
                                     scale=-0.5).then_inc(s2["si"], 1)
                    if st >= 1:
                        sb = st - 1
                        for dt in range(NDT):
                            g = sb * NDT + dt
                            s.wait_ge(s2["vmx"], g + 1)
                            if sb >= 1:
                                s.wait_ge(s2["g2m"], g - NDT + 1)
                                s.wait_ge(s2["vmo"], g - NDT + 1)
                            s.activation(h2buf[dt][:, 1:TS + 1], t2m[dt % 2],
                                         AF.Identity,
                                         scale=vc["ln2g"][:, dt:dt + 1],
                                         bias=vc["ln2b"][:, dt:dt + 1]
                                         ).then_inc(s2["shb"], 1)
                        for ht in range(NDT):
                            hh = sb * NDT + ht
                            p = ht % 2
                            s.wait_ge(s2["mmR"], hh + 1)
                            if sb >= 2 and ht == 0:
                                s.wait_ge(sstk[sb % 2],
                                          16 * 2 * NDT * ((sb - 2) // 2 + 1))
                            s.activation(sg21[p], pr2[p], AF.Exp, scale=-1.0
                                         ).then_inc(s2["sg"], 1)
                            s.activation(sg22[p], sg21[p], AF.Ln,
                                         bias=onecol).then_inc(s2["sg"], 1)
                            s.activation(rcb[sb % 2][ht], sg22[p], AF.Exp,
                                         scale=-1.0).then_inc(s2["sg"], 1)

            @blk.tensor
            def _(t):
                t.wait_ge(s2["cv"], 2 * NDT)
                for st in range(NST + 1):
                    if st < NST:
                        t.wait_ge(s2ld[st % 2], 16 * 2 * NDT * (st // 2 + 1))
                        for et in range(NDT):
                            ee = st * NDT + et
                            if ee >= 2:
                                t.wait_ge(s2["vx"], ee - 1)
                            for dt in range(NDT):
                                ins = t.matmul(
                                    pd2[et % 2],
                                    wob[dt][:, et * 128:(et + 1) * 128],
                                    yy2[st % 2][dt], start=(dt == 0),
                                    stop=(dt == NDT - 1))
                            ins.then_inc(s2["mmD"], 1)
                        t.wait_ge(s2["vx"], NDT * (st + 1))
                        if st > 0:
                            t.wait_ge(s2["vv"], st)
                        for dt in range(NDT):
                            ins = t.matmul(pmu2[:], cst[:], x2b[st % 2][dt],
                                           start=(dt == 0),
                                           stop=(dt == NDT - 1))
                        ins.then_inc(s2["mmS"], 1)
                        for dt in range(NDT):
                            t.wait_ge(s2["vq"], st * NDT + dt + 1)
                            ins = t.matmul(pe22[:], cst[:], xsq2[dt],
                                           start=(dt == 0),
                                           stop=(dt == NDT - 1))
                        ins.then_inc(s2["mmS"], 1)
                    if st >= 1:
                        sb = st - 1
                        t.wait_ge(s2["vmo"], NDT * (sb + 1))
                        for ht in range(NDT):
                            hh = sb * NDT + ht
                            if hh >= 2:
                                t.wait_ge(s2["sg"], 3 * (hh - 2) + 1)
                            for dt in range(NDT):
                                ins = t.matmul(
                                    pr2[ht % 2],
                                    cwrb[dt][:, ht * 128:(ht + 1) * 128],
                                    xr2b[dt], start=(dt == 0),
                                    stop=(dt == NDT - 1))
                            ins.then_inc(s2["mmR"], 1)

            @blk.gpsimd
            def _(gp):
                for st in range(NST + 1):
                    if st < NST:
                        for dt in range(NDT):
                            gp.wait_ge(s2["vx"], st * NDT + dt + 1)
                            gp.dma_start(x2_s[st, dt], x2b[st % 2][dt]
                                         ).then_inc(sstx[st % 2], 16)
                    if st >= 1:
                        sb = st - 1
                        gp.wait_ge(s2["vps"], sb + 1)
                        if sb >= 1:
                            gp.wait_ge(s2["vmx"], NDT * sb)
                        for dt in range(NDT):
                            gp.tensor_tensor(t2g[dt], x2b[sb % 2][dt],
                                             musb2[sb % 2], OP.subtract
                                             ).then_inc(s2["g1m"], 1)
                        for dt in range(NDT):
                            g = sb * NDT + dt
                            gp.wait_ge(s2["shb"], g + 1)
                            if g >= 2:
                                gp.wait_ge(s2["vmo"], g - 1)
                            gp.tensor_tensor(dmix2[dt % 2],
                                             h2buf[dt][:, 1:TS + 1],
                                             h2buf[dt][:, 0:TS], OP.subtract
                                             ).then_inc(s2["g2m"], 1)
                        for dt in range(NDT):
                            gp.wait_ge(s2["vmo"], sb * NDT + dt + 1)
                            gp.dma_start(xk2_s[sb, dt], xk2b[sb % 2][dt]
                                         ).then_inc(sstk[sb % 2], 16)
                        for dt in range(NDT):
                            gp.wait_ge(s2["sg"], 3 * (sb * NDT + dt) + 3)
                            gp.dma_start(rc_s[sb, dt], rcb[sb % 2][dt]
                                         ).then_inc(sstk[sb % 2], 16)
                gp.wait_ge(sstx[0], 16 * NDT * ((NST + 1) // 2))
                gp.wait_ge(sstk[0], 16 * 2 * NDT * ((NST + 1) // 2))
                if NST > 1:
                    gp.wait_ge(sstx[1], 16 * NDT * (NST // 2))
                    gp.wait_ge(sstk[1], 16 * 2 * NDT * (NST // 2))

    s3 = {n: SEM("s3" + n) for n in
          ["cv", "mmK", "mmD", "vk", "sq", "va", "tp", "ev", "vps"]}
    s3ld = [SEM("s3lda"), SEM("s3ldb")]
    s3wl = [SEM("s3wla"), SEM("s3wlb")]
    s3st = [SEM("s3sta"), SEM("s3stb")]
    with ExitStack() as es:
        S = lambda nm, sh, dt=F32: es.enter_context(nc.sbuf_tensor(nm, sh, dt))[:]
        PS = lambda nm, sh, dt=F32: es.enter_context(nc.psum_tensor(nm, sh, dt))[:]
        wstg3 = [S(f"w3stg{i}", [128, D]) for i in range(2)]
        cwkb = [S(f"cwkb{i}", [128, FF], BF16) for i in range(NDT)]
        cwvb = [S(f"cwvb{i}", [128, D], BF16) for i in range(NFT)]
        xk3 = [[S(f"xk3{p}_{d}", [128, TS3], BF16) for d in range(NDT)]
               for p in range(2)]
        x23 = [[S(f"x23{p}_{d}", [128, TS3], BF16) for d in range(NDT)]
               for p in range(2)]
        rc3 = [[S(f"rc3{p}_{d}", [128, TS3], BF16) for d in range(NDT)]
               for p in range(2)]
        relu3 = [S(f"relu3{p}", [128, TS3]) for p in range(2)]
        kcb = [S(f"kcb{f}", [128, TS3], BF16) for f in range(NFT)]
        atb = [S(f"atb{p}", [128, TS3], BF16) for p in range(2)]
        atok = [[S(f"atok{p}_{b}", [128, D], BF16) for b in range(NT3)]
                for p in range(2)]
        xx3 = [[S(f"xx3{p}_{b}", [128, D]) for b in range(NT3)]
               for p in range(2)]
        pkc = [PS(f"pkc{p}", [128, TS3]) for p in range(2)]
        pdl = [PS(f"pdl{p}", [128, TS3]) for p in range(2)]
        ptp3 = [PS(f"ptp3{p}", [128, 512], BF16) for p in range(2)]
        def _ev3(v, st, et):
            ee = st * NDT + et
            v.wait_ge(s3["tp"], ee + 1)
            if et == 0 and st >= 2:
                v.wait_ge(s3st[st % 2], 16 * NT3 * ((st - 2) // 2 + 1))
            for tb in range(NT3):
                ins = v.tensor_tensor(
                    atok[st % 2][tb][:, et * 128:(et + 1) * 128],
                    ptp3[et % 2][:, tb * 128:(tb + 1) * 128],
                    xx3[st % 2][tb][:, et * 128:(et + 1) * 128], OP.add)
            ins.then_inc(s3["ev"], 1)

        with nc.Block() as blk:
            @blk.sync
            def _(s):
                for j in range(4 * NDT + NFT):
                    if j >= 2:
                        s.wait_ge(s3["cv"], j - 1)
                    if j < 4 * NDT:
                        dt, c = divmod(j, 4)
                        ap = w_h["cwk"][dt * 128:(dt + 1) * 128,
                                        c * D:(c + 1) * D]
                    else:
                        ft = j - 4 * NDT
                        ap = w_h["cwv"][ft * 128:(ft + 1) * 128, :]
                    s.dma_start(wstg3[j % 2], ap).then_inc(s3wl[j % 2], 16)
                for st in range(NS3):
                    if st >= 2:
                        s.wait_ge(s3["va"], NDT * (st - 1))
                    q, r = divmod(st, 2)
                    for tb in range(NT3):
                        tt = st * NT3 + tb
                        s.dma_start(xx3[st % 2][tb],
                                    x_h[tt * 128:(tt + 1) * 128, :]
                                    ).then_inc(s3ld[st % 2], 16)
                    for dt in range(NDT):
                        s.dma_start(xk3[st % 2][dt],
                                    xk2_s[q, dt][:, r * TS3:(r + 1) * TS3]
                                    ).then_inc(s3ld[st % 2], 16)
                    for dt in range(NDT):
                        s.dma_start(x23[st % 2][dt],
                                    x2_s[q, dt][:, r * TS3:(r + 1) * TS3]
                                    ).then_inc(s3ld[st % 2], 16)
                    for dt in range(NDT):
                        s.dma_start(rc3[st % 2][dt],
                                    rc_s[q, dt][:, r * TS3:(r + 1) * TS3]
                                    ).then_inc(s3ld[st % 2], 16)

            @blk.vector
            def _(v):
                for j in range(4 * NDT + NFT):
                    v.wait_ge(s3wl[j % 2], 16 * (j // 2 + 1))
                    if j < 4 * NDT:
                        dt, c = divmod(j, 4)
                        dst = cwkb[dt][:, c * D:(c + 1) * D]
                    else:
                        dst = cwvb[j - 4 * NDT]
                    v.tensor_copy(dst, wstg3[j % 2]).then_inc(s3["cv"], 1)
                for st in range(NS3):
                    for ft in range(NFT):
                        ff = st * NFT + ft
                        v.wait_ge(s3["mmK"], ff + 1)
                        if ff >= 2:
                            v.wait_ge(s3["sq"], ff - 1)
                        v.tensor_scalar(relu3[ft % 2], pkc[ft % 2], 0.0, None,
                                        OP.max).then_inc(s3["vk"], 1)
                    for et in range(NDT):
                        ee = st * NDT + et
                        v.wait_ge(s3["mmD"], ee + 1)
                        if ee >= 2:
                            v.wait_ge(s3["tp"], ee - 1)
                        v.tensor_tensor(atb[et % 2], rc3[st % 2][et],
                                        pdl[et % 2], OP.mult
                                        ).then_inc(s3["vps"], 1)
                        v.wait_ge(s3["vps"], ee + 1)
                        v.tensor_tensor(atb[et % 2], atb[et % 2],
                                        x23[st % 2][et], OP.add
                                        ).then_inc(s3["va"], 1)
                        if et > 0:
                            _ev3(v, st, et - 1)
                    _ev3(v, st, NDT - 1)

            @blk.scalar
            def _(s):
                for st in range(NS3):
                    for ft in range(NFT):
                        ff = st * NFT + ft
                        s.wait_ge(s3["vk"], ff + 1)
                        if ft == 0 and st >= 1:
                            s.wait_ge(s3["mmD"], NDT * st)
                        s.activation(kcb[ft], relu3[ft % 2], AF.Square
                                     ).then_inc(s3["sq"], 1)

            @blk.tensor
            def _(t):
                t.wait_ge(s3["cv"], 4 * NDT + NFT)
                for st in range(NS3):
                    t.wait_ge(s3ld[st % 2],
                              16 * (3 * NDT + NT3) * (st // 2 + 1))
                    for ft in range(NFT):
                        ff = st * NFT + ft
                        if ff >= 2:
                            t.wait_ge(s3["vk"], ff - 1)
                        for dt in range(NDT):
                            ins = t.matmul(pkc[ft % 2],
                                           cwkb[dt][:, ft * 128:(ft + 1) * 128],
                                           xk3[st % 2][dt], start=(dt == 0),
                                           stop=(dt == NDT - 1))
                        ins.then_inc(s3["mmK"], 1)
                    def _tp3(et):
                        ee = st * NDT + et
                        t.wait_ge(s3["va"], ee + 1)
                        if ee >= 2:
                            t.wait_ge(s3["ev"], ee - 1)
                        for tb in range(NT3):
                            ins = t.transpose(
                                ptp3[et % 2][:, tb * 128:(tb + 1) * 128],
                                atb[et % 2][:, tb * 128:(tb + 1) * 128],
                                idb[:])
                        ins.then_inc(s3["tp"], 1)

                    for et in range(NDT):
                        ee = st * NDT + et
                        if ee >= 2:
                            t.wait_ge(s3["va"], ee - 1)
                        for ft in range(NFT):
                            t.wait_ge(s3["sq"], st * NFT + ft + 1)
                            ins = t.matmul(pdl[et % 2],
                                           cwvb[ft][:, et * 128:(et + 1) * 128],
                                           kcb[ft], start=(ft == 0),
                                           stop=(ft == NFT - 1))
                        ins.then_inc(s3["mmD"], 1)
                        if et > 0:
                            _tp3(et - 1)
                    _tp3(NDT - 1)

            @blk.gpsimd
            def _(gp):
                for st in range(NS3):
                    gp.wait_ge(s3["ev"], NDT * (st + 1))
                    for tb in range(NT3):
                        gp.dma_start(at_s[st * NT3 + tb], atok[st % 2][tb]
                                     ).then_inc(s3st[st % 2], 16)
                gp.wait_ge(s3st[0], 16 * NT3 * ((NS3 + 1) // 2))
                if NS3 > 1:
                    gp.wait_ge(s3st[1], 16 * NT3 * (NS3 // 2))

    es45 = ExitStack()
    w1tT = [es45.enter_context(nc.sbuf_tensor(f"w1tT{i}", [128, FF], BF16))[:]
            for i in range(NDT)]
    w2tT = [es45.enter_context(nc.sbuf_tensor(f"w2tT{i}", [128, D], BF16))[:]
            for i in range(NFT)]

    s4 = {n: SEM("s4" + n) for n in ["ra", "gs", "tn", "tp", "ev", "ms"]}
    s4ld = [SEM("s4lda"), SEM("s4ldb")]
    s4l2 = [SEM("s4l2a"), SEM("s4l2b")]
    mats = [("w1", NFT, D, NDT, w1tT), ("w2", NDT, FF, NFT, w2tT)]
    with ExitStack() as es:
        S = lambda nm, sh, dt=F32: es.enter_context(nc.sbuf_tensor(nm, sh, dt))[:]
        PS = lambda nm, sh, dt=F32: es.enter_context(nc.psum_tensor(nm, sh, dt))[:]
        wstg4 = [S(f"w4stg{i}", [128, FF]) for i in range(2)]
        racc = S("racc", [128, NFT + 2], F32)
        thr = S("thr", [128, 4], F32)
        onesf = S("onesf", [128, 128], F32)
        pthr = PS("pthr", [128, 2])
        s4on = SEM("s4on")
        tb1 = S("tb1", [128, FF], BF16)
        tb2 = S("tb2", [128, FF], BF16)
        wtb = [S(f"wtb{i}", [128, FF], BF16) for i in range(2)]
        ptp4 = [PS(f"ptp4{p}", [128, 512], BF16) for p in range(2)]

        def blocks_before(mi, it):
            b = 0
            for m in range(mi):
                b += mats[m][1] * mats[m][3]
            return b + it * mats[mi][3]

        with nc.Block() as blk:
            @blk.sync
            def _(s):
                s.dma_start(onesf, onesf_h).then_inc(s4on, 16)
                for mi, (nme, ntl, wid, nbk, dst) in enumerate(mats):
                    base = 0 if mi == 0 else mats[0][1]
                    for it in range(ntl):
                        if base + it >= 2:
                            s.wait_ge(s4["ra"],
                                      base + (1 if mi else 0) + it - 1)
                        if mi == 1 and it < 2:
                            s.wait_ge(s4["tn"], mats[0][1])
                        s.dma_start(wstg4[it % 2][:, 0:wid],
                                    w_h[nme][it * 128:(it + 1) * 128, :]
                                    ).then_inc(s4ld[(base + it) % 2], 16)
                    s.wait_ge(s4["gs"], mi + 1)
                    for it in range(ntl):
                        if it >= 2:
                            s.wait_ge(s4["tn"], base + it - 1)
                        s.dma_start(wstg4[it % 2][:, 0:wid],
                                    w_h[nme][it * 128:(it + 1) * 128, :]
                                    ).then_inc(s4l2[(base + it) % 2], 16)

            @blk.vector
            def _(v):
                for mi, (nme, ntl, wid, nbk, dst) in enumerate(mats):
                    base = 0 if mi == 0 else mats[0][1]
                    for it in range(ntl):
                        v.wait_ge(s4ld[(base + it) % 2],
                                  16 * ((base + it) // 2 + 1))
                        v.reduce_sum(racc[:, it:it + 1], wstg4[it % 2][:, 0:wid],
                                     axis=AX, apply_absolute_value=True
                                     ).then_inc(s4["ra"], 1)
                    v.wait_ge(s4["ra"], base + (1 if mi else 0) + ntl)
                    v.reduce_sum(racc[:, NFT + mi:NFT + mi + 1],
                                 racc[:, 0:ntl], axis=AX).then_inc(s4["ra"], 1)
                    v.wait_ge(s4["ms"], mi + 1)
                    v.tensor_scalar(thr[:, mi:mi + 1], pthr[:, mi:mi + 1],
                                    0.5 / (FF * D), None, OP.mult)
                    v.tensor_scalar(thr[:, 2 + mi:3 + mi], pthr[:, mi:mi + 1],
                                    -0.5 / (FF * D), None, OP.mult
                                    ).then_inc(s4["gs"], 1)
                    for it in range(ntl):
                        v.wait_ge(s4l2[(base + it) % 2],
                                  16 * ((base + it) // 2 + 1))
                        if it >= 2:
                            v.wait_ge(s4["tp"], blocks_before(mi, it - 1))
                        elif mi == 1:
                            v.wait_ge(s4["tp"], blocks_before(0, mats[0][1]))
                        v.tensor_single_scalar(tb1[:, 0:wid],
                                               wstg4[it % 2][:, 0:wid],
                                               thr[:, mi:mi + 1], OP.is_gt)
                        v.tensor_single_scalar(tb2[:, 0:wid],
                                               wstg4[it % 2][:, 0:wid],
                                               thr[:, 2 + mi:3 + mi], OP.is_lt)
                        v.tensor_tensor(wtb[it % 2][:, 0:wid], tb1[:, 0:wid],
                                        tb2[:, 0:wid], OP.subtract
                                        ).then_inc(s4["tn"], 1)

            @blk.tensor
            def _(t):
                t.wait_ge(s4on, 16)
                nblk = 0
                for mi, (nme, ntl, wid, nbk, dst) in enumerate(mats):
                    base = 0 if mi == 0 else mats[0][1]
                    need = (mats[0][1] + 1) if mi == 0 else \
                        (mats[0][1] + mats[1][1] + 2)
                    t.wait_ge(s4["ra"], need)
                    t.matmul(pthr[:, mi:mi + 1], onesf,
                             racc[:, NFT + mi:NFT + mi + 1],
                             start=True, stop=True).then_inc(s4["ms"], 1)
                    for it in range(ntl):
                        t.wait_ge(s4["tn"], base + it + 1)
                        for c in range(nbk):
                            if nblk % 4 == 0 and nblk >= 8:
                                t.wait_ge(s4["ev"], nblk - 4)
                            ins = t.transpose(
                                ptp4[(nblk // 4) % 2]
                                [:, (nblk % 4) * 128:(nblk % 4 + 1) * 128],
                                wtb[it % 2][:, c * 128:(c + 1) * 128], idb[:])
                            if nblk % 4 == 3:
                                ins.then_inc(s4["tp"], 4)
                            nblk += 1

            @blk.scalar
            def _(s):
                nblk = 0
                for mi, (nme, ntl, wid, nbk, dst) in enumerate(mats):
                    for it in range(ntl):
                        for c in range(nbk):
                            if nblk % 4 == 0:
                                s.wait_ge(s4["tp"], nblk + 4)
                            ins = s.copy(dst[c][:, it * 128:(it + 1) * 128],
                                         ptp4[(nblk // 4) % 2]
                                         [:, (nblk % 4) * 128:
                                          (nblk % 4 + 1) * 128])
                            if nblk % 4 == 3:
                                ins.then_inc(s4["ev"], 4)
                            nblk += 1

    TS5 = 512
    NS5 = T // TS5
    NT5 = TS5 // 128
    s5 = {n: SEM("s5" + n) for n in ["mmF", "mmO", "sg", "so"]}
    s5ld = [SEM("s5lda"), SEM("s5ldb")]
    s5st = [SEM("s5sta"), SEM("s5stb")]
    with ExitStack() as es:
        S = lambda nm, sh, dt=F32: es.enter_context(nc.sbuf_tensor(nm, sh, dt))[:]
        PS = lambda nm, sh, dt=F32: es.enter_context(nc.psum_tensor(nm, sh, dt))[:]
        xn5 = [[S(f"xn5{p}_{d}", [128, TS5], BF16) for d in range(NDT)]
               for p in range(2)]
        at5 = [[S(f"at5{p}_{b}", [128, D], BF16) for b in range(NT5)]
               for p in range(2)]
        glb = [S(f"glb{f}", [128, TS5], BF16) for f in range(NFT)]
        outb = [S(f"outb{b}", [128, 512]) for b in range(4)]
        pff = [PS(f"pff{p}", [128, TS5]) for p in range(2)]
        pout = [PS(f"pout{p}", [128, 512]) for p in range(2)]
        with nc.Block() as blk:
            @blk.sync
            def _(s):
                for st in range(NS5):
                    if st >= 2:
                        s.wait_ge(s5["so"], 2 * NT5 * (st - 1))
                    for dt in range(NDT):
                        s.dma_start(xn5[st % 2][dt], xn_s[st, dt]
                                    ).then_inc(s5ld[st % 2], 16)
                    for tb in range(NT5):
                        tt = st * NT5 + tb
                        s.dma_start(at5[st % 2][tb], at_s[tt]
                                    ).then_inc(s5ld[st % 2], 16)

            @blk.tensor
            def _(t):
                for st in range(NS5):
                    t.wait_ge(s5ld[st % 2], 16 * (NDT + NT5) * (st // 2 + 1))
                    for ft in range(NFT):
                        ff = st * NFT + ft
                        if ff >= 2:
                            t.wait_ge(s5["sg"], ff - 1)
                        for dt in range(NDT):
                            ins = t.matmul(pff[ft % 2],
                                           w1tT[dt][:, ft * 128:(ft + 1) * 128],
                                           xn5[st % 2][dt], start=(dt == 0),
                                           stop=(dt == NDT - 1))
                        ins.then_inc(s5["mmF"], 1)
                    t.wait_ge(s5["sg"], (st + 1) * NFT)
                    for tb in range(NT5):
                        oo = st * NT5 * 2 + tb * 2
                        if oo >= 2:
                            t.wait_ge(s5["so"], oo)
                        for ft in range(NFT):
                            for ec in range(2):
                                t.matmul(pout[ec],
                                         glb[ft][:, tb * 128:(tb + 1) * 128],
                                         w2tT[ft][:, ec * 512:(ec + 1) * 512],
                                         start=(ft == 0), stop=False,
                                         skip_group_check=True)
                        for ec in range(2):
                            t.matmul(pout[ec], onr[:],
                                     b2r[:, ec * 512:(ec + 1) * 512],
                                     start=False, stop=False,
                                     skip_group_check=True)
                            ins = t.matmul(
                                pout[ec], idb[:],
                                at5[st % 2][tb][:, ec * 512:(ec + 1) * 512],
                                start=False, stop=True,
                                skip_group_check=True)
                        ins.then_inc(s5["mmO"], 2)

            @blk.scalar
            def _(s):
                for st in range(NS5):
                    for ft in range(NFT):
                        ff = st * NFT + ft
                        s.wait_ge(s5["mmF"], ff + 1)
                        if ft == 0 and st >= 1:
                            s.wait_ge(s5["mmO"], 2 * NT5 * st)
                        s.activation(glb[ft], pff[ft % 2], AF.Gelu,
                                     bias=b1c[:, ft:ft + 1]).then_inc(s5["sg"], 1)
                    for tb in range(NT5):
                        for ec in range(2):
                            oo = st * NT5 * 2 + tb * 2 + ec
                            j = tb * 2 + ec
                            s.wait_ge(s5["mmO"], st * NT5 * 2 + tb * 2 + 2)
                            if st >= 1 and j == 0:
                                s.wait_ge(s5st[(st - 1) % 2],
                                          16 * 2 * NT5 * ((st - 1) // 2 + 1))
                            if j >= 4:
                                s.wait_ge(s5st[st % 2],
                                          16 * (2 * NT5 * (st // 2) + j - 3))
                            s.copy(outb[j % 4], pout[ec]
                                   ).then_inc(s5["so"], 1)

            @blk.gpsimd
            def _(gp):
                for st in range(NS5):
                    for tb in range(NT5):
                        for ec in range(2):
                            oo = st * NT5 * 2 + tb * 2 + ec
                            tt = st * NT5 + tb
                            gp.wait_ge(s5["so"], oo + 1)
                            gp.dma_start(
                                y_h[tt * 128:(tt + 1) * 128,
                                    ec * 512:(ec + 1) * 512],
                                outb[(tb * 2 + ec) % 4]
                                ).then_inc(s5st[st % 2], 16)
                gp.wait_ge(s5st[0], 16 * 2 * NT5 * ((NS5 + 1) // 2))
                if NS5 > 1:
                    gp.wait_ge(s5st[1], 16 * 2 * NT5 * (NS5 // 2))

    es45.close()
    return nc


def make_inputs_per_core(inp, xc):
    g = lambda n: np.ascontiguousarray(np.asarray(inp["L0_attn_" + n],
                                                  dtype=np.float32))
    m = {
        "x": np.ascontiguousarray(np.asarray(xc, dtype=np.float32)),
        "wk": g("tm_Wk"), "wv": g("tm_Wv"), "wr": g("tm_Wr"), "wo": g("Wo"),
        "cwr": g("cm_Wr"), "cwk": g("cm_Wk"), "cwv": g("cm_Wv"),
        "w1": np.ascontiguousarray(np.asarray(inp["L0_ffn_W1"], np.float32)),
        "w2": np.ascontiguousarray(np.asarray(inp["L0_ffn_W2"], np.float32)),
        "ln1g": g("ln1_g"), "ln1b": g("ln1_b"), "ln2g": g("ln2_g"),
        "ln2b": g("ln2_b"), "mk": g("tm_mix_k"), "mv": g("tm_mix_v"),
        "mr": g("tm_mix_r"), "td": g("time_decay"), "tf": g("time_first"),
        "cmk": g("cm_mix_k"), "cmr": g("cm_mix_r"),
        "b1": np.ascontiguousarray(np.asarray(inp["L0_ffn_b1"], np.float32)),
        "b2": np.ascontiguousarray(np.asarray(inp["L0_ffn_b2"], np.float32)),
        "idf_h": np.eye(128, dtype=np.float32),
        "cc_h": np.array([[1.0, 1e-3, 1e-5, 0.0]] * 128, dtype=np.float32),
        "onesf_h": np.ones((128, 128), dtype=np.float32),
        "onesb_h": np.ones((128, 512), dtype=bf16),
        "idb_h": np.eye(128, dtype=bf16),
        "cst_h": np.full((128, 128), 1.0 / 1024, dtype=bf16),
        "onr_h": np.ones((1, 128), dtype=bf16),
    }
    return m


def kernel(**inputs):
    x = np.asarray(inputs["L0_attn_inputs"])
    nc = build(T=x.shape[1])
    in_maps = [make_inputs_per_core(inputs, x[c]) for c in range(8)]
    res = run_bass_kernel_spmd(nc, in_maps, core_ids=list(range(8)))
    return np.stack([np.asarray(res.results[c]["y"]) for c in range(8)], axis=0)


# version 07 (final)
# speedup vs baseline: 1.0849x; 1.0849x over previous
import numpy as np
import ml_dtypes
from contextlib import ExitStack
import concourse.bass as bass
import concourse.mybir as mybir
from bass_rust import ReduceOp
from concourse.bass_utils import run_bass_kernel_spmd

F32 = mybir.dt.float32
BF16 = mybir.dt.bfloat16
AF = mybir.ActivationFunctionType
OP = mybir.AluOpType
AX = mybir.AxisListType.X
bf16 = ml_dtypes.bfloat16

D = 1024
FF = 4096
NDT = D // 128
NFT = FF // 128
TS = 512
TS3 = 256


def build(T=8192, dump=False):
    NST = T // TS
    NS3 = T // TS3
    NTT = T // 128
    NT3 = TS3 // 128
    nc = bass.Bass("TRN2", target_bir_lowering=False,
                   detect_race_conditions=False)

    x_h = nc.dram_tensor("x", [T, D], F32, kind="ExternalInput").ap()
    y_h = nc.dram_tensor("y", [T, D], F32, kind="ExternalOutput").ap()
    w_h = {}
    for name, shp in [("wk", [D, D]), ("wv", [D, D]), ("wr", [D, D]),
                      ("wo", [D, D]), ("cwr", [D, D]), ("cwk", [D, FF]),
                      ("cwv", [FF, D]), ("w1", [FF, D]), ("w2", [D, FF])]:
        w_h[name] = nc.dram_tensor(name, shp, F32, kind="ExternalInput").ap()
    v_h = {}
    for name in ["ln1g", "ln1b", "ln2g", "ln2b", "mk", "mv", "mr", "td",
                 "tf", "cmk", "cmr", "b2"]:
        v_h[name] = nc.dram_tensor(name, [D], F32, kind="ExternalInput").ap()
    v_h["b1"] = nc.dram_tensor("b1", [FF], F32, kind="ExternalInput").ap()
    idf_h = nc.dram_tensor("idf_h", [128, 128], F32, kind="ExternalInput").ap()
    idb_h = nc.dram_tensor("idb_h", [128, 128], BF16, kind="ExternalInput").ap()
    cst_h = nc.dram_tensor("cst_h", [128, 128], BF16, kind="ExternalInput").ap()
    onr_h = nc.dram_tensor("onr_h", [1, 128], BF16, kind="ExternalInput").ap()
    cc_h = nc.dram_tensor("cc_h", [128, 4], F32, kind="ExternalInput").ap()
    onesf_h = nc.dram_tensor("onesf_h", [128, 128], F32,
                             kind="ExternalInput").ap()
    onesb_h = nc.dram_tensor("onesb_h", [128, 512], BF16,
                             kind="ExternalInput").ap()

    skind = "ExternalOutput" if dump else "Internal"
    xn_s = nc.dram_tensor("xn_s", [NST, NDT, 128, TS], BF16, kind=skind).ap()
    y1_s = nc.dram_tensor("y1_s", [NST, NDT, 128, TS], BF16, kind=skind).ap()
    x2_s = nc.dram_tensor("x2_s", [NST, NDT, 128, TS], BF16, kind=skind).ap()
    rc_s = nc.dram_tensor("rc_s", [NST, NDT, 128, TS], BF16, kind=skind).ap()
    xk2_s = nc.dram_tensor("xk2_s", [NST, NDT, 128, TS], BF16, kind=skind).ap()
    at_s = nc.dram_tensor("at_s", [NTT, 128, D], BF16, kind=skind).ap()

    A = nc.alloc_sbuf_tensor
    idf = A("idf", [128, 128], F32)
    idb = A("idb", [128, 128], BF16)
    cst = A("cst", [128, 128], BF16)
    onr = A("onr", [1, 128], BF16)
    VC = ["ln1g", "ln1b", "ln2g", "ln2b", "mk", "mv", "mr", "cmk", "cmr",
          "dk", "c1", "c2"]
    vc = {n: A("vc_" + n, [128, NDT], F32) for n in VC}
    b1c = A("b1c", [128, NFT], F32)
    b2r = A("b2r", [1, D], BF16)
    cc = A("cc", [128, 4], F32)
    onesb = A("onesb", [128, 512], BF16)
    onecol = cc[:, 0:1]
    eps3 = cc[:, 1:2]
    eps5 = cc[:, 2:3]
    zcol = cc[:, 3:4]
    SEM = nc.alloc_semaphore

    p_c, p_v = SEM("p_c"), SEM("p_v")
    p_ts, p_vs, p_sc = SEM("p_ts"), SEM("p_vs"), SEM("p_sc")
    vec_list = ["ln1g", "ln1b", "ln2g", "ln2b", "mk", "mv", "mr",
                "cmk", "cmr", "td", "tf"]
    NV = len(vec_list)
    with ExitStack() as es:
        vstage = es.enter_context(nc.sbuf_tensor("vstage", [32, 128], F32))[:]
        b2stage = es.enter_context(nc.sbuf_tensor("b2stage", [1, D], F32))[:]
        pvc = es.enter_context(nc.psum_tensor("pvc", [128, 32], F32))[:]
        with nc.Block() as blk:
            @blk.sync
            def _(s):
                s.dma_start(idf[:], idf_h).then_inc(p_c, 16)
                s.dma_start(idb[:], idb_h).then_inc(p_c, 16)
                s.dma_start(cst[:], cst_h).then_inc(p_c, 16)
                s.dma_start(onr[:], onr_h).then_inc(p_c, 16)
                s.dma_start(b2stage[:], v_h["b2"].rearrange("(a d) -> a d", a=1)
                            ).then_inc(p_c, 16)
                s.dma_start(cc[:], cc_h).then_inc(p_c, 16)
                s.dma_start(onesb[:], onesb_h).then_inc(p_c, 16)
                for i, n in enumerate(vec_list):
                    if i >= 1:
                        s.wait_ge(p_ts, i)
                    s.dma_start(vstage[0:NDT, :],
                                v_h[n].rearrange("(a b) -> a b", b=128)
                                ).then_inc(p_v, 16)
                s.wait_ge(p_ts, NV)
                s.dma_start(vstage[:, :],
                            v_h["b1"].rearrange("(a b) -> a b", b=128)
                            ).then_inc(p_v, 16)

            @blk.tensor
            def _(t):
                t.wait_ge(p_c, 16 * 7)
                for i in range(NV):
                    t.wait_ge(p_v, 16 * (i + 1))
                    if i >= 1:
                        t.wait_ge(p_vs, i)
                    t.transpose(pvc[:, 0:NDT], vstage[0:NDT, :],
                                idf[0:NDT, 0:NDT]).then_inc(p_ts, 1)
                t.wait_ge(p_v, 16 * (NV + 1))
                t.wait_ge(p_vs, NV)
                t.transpose(pvc[:, 0:32], vstage[:, :], idf[0:32, 0:32]
                            ).then_inc(p_ts, 1)

            @blk.vector
            def _(v):
                for i, n in enumerate(vec_list):
                    v.wait_ge(p_ts, i + 1)
                    dst = {"td": vc["dk"], "tf": vc["c2"]}.get(n, vc.get(n))
                    v.tensor_copy(dst[:, 0:NDT], pvc[:, 0:NDT]).then_inc(p_vs, 1)
                v.wait_ge(p_ts, NV + 1)
                v.tensor_copy(b1c[:], pvc[:, 0:32]).then_inc(p_vs, 1)
                v.wait_ge(p_c, 16 * 7)
                v.tensor_copy(b2r[:], b2stage[:])
                v.wait_ge(p_sc, 1)
                v.tensor_tensor(vc["c2"][:], vc["c2"][:], vc["c1"][:],
                                OP.subtract)
                v.tensor_tensor(vc["c1"][:], vc["c2"][:], vc["dk"][:],
                                OP.mult)

            @blk.scalar
            def _(s):
                s.wait_ge(p_vs, NV)
                s.activation(vc["c1"][:], vc["dk"][:], AF.Exp)
                s.activation(vc["dk"][:], vc["c1"][:], AF.Exp, scale=-1.0)
                s.activation(vc["c1"][:], vc["c1"][:], AF.Exp)
                s.activation(vc["c2"][:], vc["c2"][:], AF.Exp
                             ).then_inc(p_sc, 1)

    G0 = 4
    NG0 = NTT // G0
    s0 = {n: SEM("s0" + n) for n in ["bn", "vr", "iv", "xn", "tp", "ev"]}
    s0ld = [SEM("s0lda"), SEM("s0ldb")]
    s0st = [SEM("s0sta"), SEM("s0stb")]
    with ExitStack() as es:
        S = lambda nm, sh, dt=F32: es.enter_context(nc.sbuf_tensor(nm, sh, dt))[:]
        PS = lambda nm, sh, dt=F32: es.enter_context(nc.psum_tensor(nm, sh, dt))[:]
        xst = [S(f"xst{i}", [128, D]) for i in range(2 * G0)]
        bns = S("bns", [128, 12])
        st0 = S("st0", [128, 4 * G0])
        iv0 = S("iv0", [128, 2 * G0])
        xnb0 = [S(f"xnb0{i}", [128, D], BF16) for i in range(2)]
        xcm = [[S(f"xcm{p}_{d}", [128, TS], BF16) for d in range(NDT)]
               for p in range(2)]
        ptp0 = [[PS(f"ptp0{p}_{h}", [128, 512], BF16) for h in range(2)]
                for p in range(2)]
        with nc.Block() as blk:
            @blk.sync
            def _(s):
                for tt in range(NTT):
                    g = tt // G0
                    if tt >= 2 * G0:
                        s.wait_ge(s0["xn"], tt - 2 * G0 + 1)
                    s.dma_start(xst[tt % (2 * G0)],
                                x_h[tt * 128:(tt + 1) * 128, :]
                                ).then_inc(s0ld[g % 2], 16)

            def evac_group(s, g):
                for j in range(G0):
                    tt = g * G0 + j
                    s.wait_ge(s0["tp"], tt + 1)
                    if j == 0 and g >= 2:
                        s.wait_ge(s0st[g % 2], 16 * NDT * ((g - 2) // 2 + 1))
                    for ds in range(NDT):
                        ins = s.copy(xcm[g % 2][ds][:, j * 128:(j + 1) * 128],
                                     ptp0[tt % 2][ds // 4]
                                     [:, (ds % 4) * 128:(ds % 4 + 1) * 128])
                    ins.then_inc(s0["ev"], 1)

            @blk.scalar
            def _(s):
                for g in range(NG0):
                    s.wait_ge(s0["vr"], g + 1)
                    oi = (g % 2) * G0
                    s.activation(iv0[:, oi:oi + G0], iv0[:, oi:oi + G0],
                                 AF.Ln, bias=eps5)
                    s.activation(iv0[:, oi:oi + G0], iv0[:, oi:oi + G0],
                                 AF.Exp, scale=-0.5).then_inc(s0["iv"], 1)
                    if g >= 1:
                        evac_group(s, g - 1)
                evac_group(s, NG0 - 1)

            @blk.vector
            def _(v):
                for g in range(NG0):
                    o = (g % 2) * 2 * G0
                    oi = (g % 2) * G0
                    v.wait_ge(s0ld[g % 2], 16 * G0 * (g // 2 + 1))
                    if g >= 2:
                        v.wait_ge(s0["iv"], g - 1)
                    for j in range(G0):
                        tt = g * G0 + j
                        v.bn_stats(bns[:, 0:6], xst[tt % (2 * G0)][:, 0:512]
                                   ).then_inc(s0["bn"], 1)
                        v.bn_stats(bns[:, 6:12], xst[tt % (2 * G0)][:, 512:D]
                                   ).then_inc(s0["bn"], 1)
                        v.wait_ge(s0["bn"], 3 * tt + 2)
                        v.bn_aggr(st0[:, o + 2 * j:o + 2 * j + 2], bns[:]
                                  ).then_inc(s0["bn"], 1)
                    v.wait_ge(s0["bn"], 3 * G0 * (g + 1))
                    v.tensor_scalar(
                        iv0[:, oi:oi + G0],
                        bass.AP(st0.tensor, st0.offset + o + 1,
                                [[st0.ap[0][0], 128], [2, G0]]),
                        1.0, None, OP.mult).then_inc(s0["vr"], 1)
                    v.wait_ge(s0["iv"], g + 1)
                    for j in range(G0):
                        tt = g * G0 + j
                        if tt >= 2:
                            v.wait_ge(s0["tp"], tt - 1)
                        v.tensor_scalar(xnb0[tt % 2], xst[tt % (2 * G0)],
                                        st0[:, o + 2 * j:o + 2 * j + 1],
                                        iv0[:, oi + j:oi + j + 1],
                                        OP.subtract, OP.mult
                                        ).then_inc(s0["xn"], 1)

            @blk.tensor
            def _(t):
                for tt in range(NTT):
                    t.wait_ge(s0["xn"], tt + 1)
                    if tt >= 2:
                        t.wait_ge(s0["ev"], tt - 1)
                    for ds in range(NDT):
                        ins = t.transpose(
                            ptp0[tt % 2][ds // 4]
                            [:, (ds % 4) * 128:(ds % 4 + 1) * 128],
                            xnb0[tt % 2][:, ds * 128:(ds + 1) * 128], idb[:])
                    ins.then_inc(s0["tp"], 1)

            @blk.gpsimd
            def _(gp):
                for g in range(NG0):
                    gp.wait_ge(s0["ev"], G0 * (g + 1))
                    for ds in range(NDT):
                        gp.dma_start(xn_s[g, ds], xcm[g % 2][ds]
                                     ).then_inc(s0st[g % 2], 16)
                gp.wait_ge(s0st[0], 16 * NDT * ((NG0 + 1) // 2))
                if NG0 > 1:
                    gp.wait_ge(s0st[1], 16 * NDT * (NG0 // 2))

    s1 = {n: SEM("s1" + n) for n in
          ["cv", "mmS", "mmK", "vq", "mq", "vv", "vb", "vd",
           "si", "sk", "sr", "sd", "sl", "vdq", "vsc", "vps",
           "vmx", "vmo", "shb", "sc2", "vy"]}
    s1ld = [SEM("s1lda"), SEM("s1ldb")]
    s1wl = [SEM("s1wla"), SEM("s1wlb")]
    s1st = [SEM("s1sta"), SEM("s1stb")]
    with ExitStack() as es:
        S = lambda nm, sh, dt=F32: es.enter_context(nc.sbuf_tensor(nm, sh, dt))[:]
        PS = lambda nm, sh, dt=F32: es.enter_context(nc.psum_tensor(nm, sh, dt))[:]
        wstg = [S(f"wstg{i}", [128, 512]) for i in range(2)]
        wkb = [S(f"wkb{i}", [128, D], BF16) for i in range(NDT)]
        wvb = [S(f"wvb{i}", [128, D], BF16) for i in range(NDT)]
        wrb = [S(f"wrb{i}", [128, D], BF16) for i in range(NDT)]
        xn1 = [[S(f"xn1{p}_{d}", [128, TS], BF16) for d in range(NDT)]
               for p in range(2)]
        xsq1 = [S(f"xsq1{d}", [128, TS], BF16) for d in range(4)]
        var1 = [S(f"var1{p}", [128, TS]) for p in range(2)]
        musb1 = S("musb1", [128, TS])
        inv1 = [S(f"inv1{p}", [128, TS], BF16) for p in range(2)]
        t0f = S("t0f1", [128, TS], BF16)
        t0fm = [S(f"t0fm{p}", [128, TS], BF16) for p in range(2)]
        hbuf = [S(f"hbuf{d}", [128, TS + 1], BF16) for d in range(NDT)]
        dmix = S("dmix", [128, TS], BF16)
        xkb = [[S(f"xkb{p}_{d}", [128, TS], BF16) for d in range(NDT)]
               for p in range(2)]
        xvb = [[S(f"xvb{p}_{d}", [128, TS], BF16) for d in range(NDT)]
               for p in range(2)]
        xrb = [[S(f"xrb{p}_{d}", [128, TS], BF16) for d in range(NDT)]
               for p in range(2)]
        drep = [S(f"drep{d}", [128, TS], BF16) for d in range(NDT)]
        ekb = [S(f"ekb{p}", [128, TS], BF16) for p in range(2)]
        ekvb = [S(f"ekvb{p}", [128, TS], BF16) for p in range(2)]
        sab = [S(f"sab{d}", [128, TS], BF16) for d in range(NDT)]
        sbb = [S(f"sbb{d}", [128, TS], BF16) for d in range(NDT)]
        ekq = [S(f"ekq{p}", [128, TS], BF16) for p in range(2)]
        ekvq = [S(f"ekvq{p}", [128, TS], BF16) for p in range(2)]
        numb = [S(f"numb{p}", [128, TS], BF16) for p in range(3)]
        denb = [S(f"denb{p}", [128, TS], BF16) for p in range(2)]
        sg1 = [S(f"sg1{p}", [128, TS], BF16) for p in range(2)]
        sg2 = [S(f"sg2{p}", [128, TS], BF16) for p in range(2)]
        lnb = [S(f"lnb{p}", [128, TS], BF16) for p in range(2)]
        dqb = [S(f"dqb{p}", [128, TS], BF16) for p in range(2)]
        yscb = [S(f"yscb{p}", [128, TS], BF16) for p in range(2)]
        ybuf = [S(f"ybuf{d}", [128, TS], BF16) for d in range(4)]
        pmu1 = PS("pmu1", [128, TS])
        pe21 = PS("pe21", [128, TS])
        pk1 = [PS(f"pk1{p}", [128, TS]) for p in range(2)]
        pv1 = [PS(f"pv1{p}", [128, TS]) for p in range(2)]
        pr1 = [PS(f"pr1{p}", [128, TS]) for p in range(2)]

        def _y1v(v, hh):
            v.wait_ge(s1["sd"], hh + 1)
            if hh >= 4:
                v.wait_ge(s1st[hh % 2], 16 * ((hh - 4) // 2 + 1))
            v.tensor_tensor(ybuf[hh % 4], numb[hh % 3], yscb[hh % 2],
                            OP.mult).then_inc(s1["vy"], 1)

        def _dq1(v, hh):
            p = hh % 2
            v.wait_ge(s1["sr"], hh + 1)
            v.wait_ge(s1["sl"], hh + 1)
            if hh >= 2:
                v.wait_ge(s1["sd"], hh - 1)
            v.tensor_tensor(dqb[p], sg2[p], lnb[p], OP.add
                            ).then_inc(s1["vdq"], 1)

        def _mix3(v, st, dt):
            g = st * NDT + dt
            v.wait_ge(s1["shb"], g + 1)
            h1 = hbuf[dt][:, 0:TS]
            h_ = hbuf[dt][:, 1:TS + 1]
            v.tensor_tensor(dmix[:], h_, h1, OP.subtract)
            v.scalar_tensor_tensor(xkb[st % 2][dt], dmix[:],
                                   vc["mk"][:, dt:dt + 1], h1,
                                   op0=OP.mult, op1=OP.add)
            v.scalar_tensor_tensor(xvb[st % 2][dt], dmix[:],
                                   vc["mv"][:, dt:dt + 1], h1,
                                   op0=OP.mult, op1=OP.add)
            v.scalar_tensor_tensor(xrb[st % 2][dt], dmix[:],
                                   vc["mr"][:, dt:dt + 1], h1,
                                   op0=OP.mult, op1=OP.add
                                   ).then_inc(s1["vmo"], 1)

        with nc.Block() as blk:
            @blk.sync
            def _(s):
                for j in range(6 * NDT):
                    if j >= 2:
                        s.wait_ge(s1["cv"], j - 1)
                    i, c = divmod(j, 2)
                    nme, dt = ["wk", "wv", "wr"][i // NDT], i % NDT
                    s.dma_start(wstg[j % 2],
                                w_h[nme][dt * 128:(dt + 1) * 128,
                                         c * 512:(c + 1) * 512]
                                ).then_inc(s1wl[j % 2], 16)
                for st in range(NST):
                    if st >= 2:
                        s.wait_ge(s1["vmx"], NDT * (st - 1))
                        s.wait_ge(s1["vps"], st - 1)
                    for dt in range(NDT):
                        s.dma_start(xn1[st % 2][dt], xn_s[st, dt]
                                    ).then_inc(s1ld[st % 2], 16)

            @blk.vector
            def _(v):
                for j in range(6 * NDT):
                    v.wait_ge(s1wl[j % 2], 16 * (j // 2 + 1))
                    i, c = divmod(j, 2)
                    dst = [wkb, wvb, wrb][i // NDT][i % NDT]
                    v.tensor_copy(dst[:, c * 512:(c + 1) * 512],
                                  wstg[j % 2]).then_inc(s1["cv"], 1)
                for dt in range(NDT):
                    v.tensor_scalar(drep[dt], onesb[:, 0:TS],
                                    vc["dk"][:, dt:dt + 1], None, OP.mult)
                    v.tensor_copy(hbuf[dt][:, 0:1], zcol)
                    v.tensor_copy(sab[dt][:, TS - 1:TS], zcol)
                    v.tensor_copy(sbb[dt][:, TS - 1:TS], zcol)
                for it in range(NST + 1):
                    st = it
                    if st < NST:
                        v.wait_ge(s1ld[st % 2], 16 * NDT * (st // 2 + 1))
                        v.wait_ge(s1["mmS"], st + 1)
                        v.wait_ge(s1["mq"], NDT * (st + 1))
                        if st >= 2:
                            v.wait_ge(s1["si"], st - 1)
                        v.tensor_copy(musb1, pmu1).then_inc(s1["vps"], 1)
                        v.wait_ge(s1["vps"], st + 1)
                        v.tensor_tensor(var1[st % 2], musb1, musb1, OP.mult)
                        v.scalar_tensor_tensor(var1[st % 2], pe21, 1.0,
                                               var1[st % 2], op0=OP.mult,
                                               op1=OP.subtract
                                               ).then_inc(s1["vv"], 1)
                        v.wait_ge(s1["si"], st + 1)
                        if st >= 2:
                            v.wait_ge(s1["mmK"], 24 * (st - 1))
                        for dt in range(NDT):
                            if st * NDT + dt >= 2:
                                v.wait_ge(s1["shb"], st * NDT + dt - 1)
                            if st > 0:
                                v.tensor_copy(hbuf[dt][:, 0:1],
                                              hbuf[dt][:, TS:TS + 1])
                            v.tensor_tensor(t0f[:], xn1[st % 2][dt], musb1,
                                            OP.subtract)
                            v.tensor_tensor(t0fm[dt % 2], t0f[:],
                                            inv1[st % 2], OP.mult
                                            ).then_inc(s1["vmx"], 1)
                            if dt >= 1:
                                _mix3(v, st, dt - 1)
                        _mix3(v, st, NDT - 1)
                    if it >= 1:
                        sb = it - 1
                        for ht in range(NDT):
                            hh = sb * NDT + ht
                            p = ht % 2
                            v.wait_ge(s1["sk"], hh + 1)
                            v.wait_ge(s1["mmK"], 3 * hh + 3)
                            if hh >= 2:
                                v.wait_ge(s1["sc2"], hh - 1)
                            v.tensor_tensor(ekvb[p], ekb[p], pv1[p], OP.mult
                                            ).then_inc(s1["vb"], 1)
                            v.tensor_tensor_scan(sbb[ht], drep[ht], ekb[p],
                                                 sbb[ht][:, TS - 1:TS],
                                                 OP.mult, OP.add
                                                 ).then_inc(s1["vsc"], 1)
                            v.tensor_tensor_scan(sab[ht], drep[ht], ekvb[p],
                                                 sab[ht][:, TS - 1:TS],
                                                 OP.mult, OP.add
                                                 ).then_inc(s1["vsc"], 1)
                            v.wait_ge(s1["vsc"], 2 * (hh + 1))
                            v.wait_ge(s1["sc2"], hh + 1)
                            if hh >= 2:
                                v.wait_ge(s1["sl"], hh - 1)
                            if hh >= 3:
                                v.wait_ge(s1["vy"], hh - 2)
                            v.tensor_tensor(numb[hh % 3], sab[ht], ekvq[p],
                                            OP.add)
                            v.tensor_tensor(denb[p], sbb[ht], ekq[p], OP.add
                                            ).then_inc(s1["vd"], 1)
                            if hh >= 1:
                                _dq1(v, hh - 1)
                            if hh >= 2:
                                _y1v(v, hh - 2)
                        if sb == NST - 1:
                            _dq1(v, NST * NDT - 1)
                            _y1v(v, NST * NDT - 2)
                            _y1v(v, NST * NDT - 1)

            @blk.scalar
            def _(s):
                for it in range(NST + 1):
                    st = it
                    if st < NST:
                        for dt in range(NDT):
                            s.wait_ge(s1ld[st % 2],
                                      16 * (NDT * (st // 2) + dt + 1))
                            jq = st * NDT + dt
                            if jq >= 4:
                                s.wait_ge(s1["mq"], jq - 3)
                            s.activation(xsq1[jq % 4], xn1[st % 2][dt],
                                         AF.Square).then_inc(s1["vq"], 1)
                        s.wait_ge(s1["vv"], st + 1)
                        if st >= 2:
                            s.wait_ge(s1["vmx"], NDT * (st - 1))
                        s.activation(var1[st % 2], var1[st % 2], AF.Ln,
                                     bias=eps3)
                        s.activation(inv1[st % 2], var1[st % 2], AF.Exp,
                                     scale=-0.5).then_inc(s1["si"], 1)
                        for dt in range(NDT):
                            hh8 = st * NDT + dt
                            s.wait_ge(s1["vmx"], hh8 + 1)
                            if st >= 1:
                                s.wait_ge(s1["vmo"], hh8 - NDT + 1)
                            s.activation(hbuf[dt][:, 1:TS + 1], t0fm[dt % 2],
                                         AF.Identity,
                                         scale=vc["ln1g"][:, dt:dt + 1],
                                         bias=vc["ln1b"][:, dt:dt + 1]
                                         ).then_inc(s1["shb"], 1)
                    if it >= 1:
                        sb = it - 1
                        for ht in range(NDT):
                            hh = sb * NDT + ht
                            p = ht % 2
                            s.wait_ge(s1["mmK"], 3 * hh + 3)
                            if hh >= 2:
                                s.wait_ge(s1["vd"], hh - 1)
                            s.activation(ekb[p], pk1[p], AF.Exp
                                         ).then_inc(s1["sk"], 1)
                            s.activation(ekq[p], ekb[p], AF.Copy,
                                         scale=vc["c1"][:, ht:ht + 1])
                            s.activation(sg1[p], pr1[p], AF.Exp, scale=-1.0)
                            if hh >= 2:
                                s.wait_ge(s1["vdq"], hh - 1)
                            s.activation(sg2[p], sg1[p], AF.Ln, bias=onecol
                                         ).then_inc(s1["sr"], 1)
                            s.wait_ge(s1["vb"], hh + 1)
                            s.activation(ekvq[p], ekvb[p], AF.Copy,
                                         scale=vc["c1"][:, ht:ht + 1]
                                         ).then_inc(s1["sc2"], 1)
                            s.wait_ge(s1["vd"], hh + 1)
                            s.activation(lnb[p], denb[p], AF.Ln
                                         ).then_inc(s1["sl"], 1)
                            if hh >= 1:
                                hq = hh - 1
                                s.wait_ge(s1["vdq"], hq + 1)
                                if hq >= 2:
                                    s.wait_ge(s1["vy"], hq - 1)
                                s.activation(yscb[hq % 2], dqb[hq % 2], AF.Exp,
                                             scale=-1.0).then_inc(s1["sd"], 1)
                hq = NST * NDT - 1
                s.wait_ge(s1["vdq"], hq + 1)
                s.activation(yscb[hq % 2], dqb[hq % 2], AF.Exp,
                             scale=-1.0).then_inc(s1["sd"], 1)

            @blk.tensor
            def _(t):
                t.wait_ge(s1["cv"], 6 * NDT)
                for it in range(NST + 1):
                    st = it
                    if st < NST:
                        t.wait_ge(s1ld[st % 2], 16 * NDT * (st // 2 + 1))
                        if st > 0:
                            t.wait_ge(s1["vv"], st)
                        for dt in range(NDT):
                            ins = t.matmul(pmu1[:], cst[:], xn1[st % 2][dt],
                                           start=(dt == 0),
                                           stop=(dt == NDT - 1))
                        ins.then_inc(s1["mmS"], 1)
                        for dt in range(NDT):
                            t.wait_ge(s1["vq"], st * NDT + dt + 1)
                            t.matmul(pe21[:], cst[:],
                                     xsq1[(st * NDT + dt) % 4],
                                     start=(dt == 0),
                                     stop=(dt == NDT - 1)
                                     ).then_inc(s1["mq"], 1)
                    if it >= 1:
                        sb = it - 1
                        t.wait_ge(s1["vmo"], NDT * (sb + 1))
                        for ht in range(NDT):
                            hh = sb * NDT + ht
                            p = ht % 2
                            if hh >= 2:
                                t.wait_ge(s1["sk"], hh - 1)
                                t.wait_ge(s1["vb"], hh - 1)
                                t.wait_ge(s1["sr"], hh - 1)
                            for dt in range(NDT):
                                t.matmul(pk1[p],
                                         wkb[dt][:, ht * 128:(ht + 1) * 128],
                                         xkb[sb % 2][dt], start=(dt == 0),
                                         stop=(dt == NDT - 1))
                            for dt in range(NDT):
                                t.matmul(pv1[p],
                                         wvb[dt][:, ht * 128:(ht + 1) * 128],
                                         xvb[sb % 2][dt], start=(dt == 0),
                                         stop=(dt == NDT - 1))
                            for dt in range(NDT):
                                ins = t.matmul(
                                    pr1[p],
                                    wrb[dt][:, ht * 128:(ht + 1) * 128],
                                    xrb[sb % 2][dt], start=(dt == 0),
                                    stop=(dt == NDT - 1))
                            ins.then_inc(s1["mmK"], 3)

            @blk.gpsimd
            def _(gp):
                for hh in range(NST * NDT):
                    st, ht = divmod(hh, NDT)
                    gp.wait_ge(s1["vy"], hh + 1)
                    gp.dma_start(y1_s[st, ht], ybuf[hh % 4]
                                 ).then_inc(s1st[hh % 2], 16)
                    gp.wait_ge(s1st[hh % 2], 16 * (hh // 2 + 1))

    s2 = {n: SEM("s2" + n) for n in
          ["cv", "mmS", "mmD", "mmR", "vx", "vq", "vv", "vps", "si", "sg",
           "vmx", "vmo", "shb"]}
    s2ld = [SEM("s2lda"), SEM("s2ldb")]
    s2wl = [SEM("s2wla"), SEM("s2wlb")]
    sstx = [SEM("sstxa"), SEM("sstxb")]
    sstk = [SEM("sstka"), SEM("sstkb")]
    with ExitStack() as es:
        S = lambda nm, sh, dt=F32: es.enter_context(nc.sbuf_tensor(nm, sh, dt))[:]
        PS = lambda nm, sh, dt=F32: es.enter_context(nc.psum_tensor(nm, sh, dt))[:]
        wstg = [S(f"w2stg{i}", [128, D]) for i in range(2)]
        wob = [S(f"wob{i}", [128, D], BF16) for i in range(NDT)]
        cwrb = [S(f"cwrb{i}", [128, D], BF16) for i in range(NDT)]
        xn2 = [[S(f"xn2{p}_{d}", [128, TS], BF16) for d in range(NDT)]
               for p in range(2)]
        yy2 = [[S(f"yy2{p}_{d}", [128, TS], BF16) for d in range(NDT)]
               for p in range(2)]
        x2b = [[S(f"x2b{p}_{d}", [128, TS], BF16) for d in range(NDT)]
               for p in range(2)]
        xsq2 = [S(f"xsq2{d}", [128, TS], BF16) for d in range(NDT)]
        var2 = [S(f"var2{p}", [128, TS]) for p in range(2)]
        musb2 = [S(f"musb2{p}", [128, TS]) for p in range(2)]
        inv2 = [S(f"inv2{p}", [128, TS], BF16) for p in range(2)]
        t2f = S("t2f", [128, TS], BF16)
        t2m = [S(f"t2m{p}", [128, TS], BF16) for p in range(2)]
        h2buf = [S(f"h2buf{d}", [128, TS + 1], BF16) for d in range(NDT)]
        dmix2 = S("dmix2", [128, TS], BF16)
        xk2b = [[S(f"xk2b{p}_{d}", [128, TS], BF16) for d in range(NDT)]
                for p in range(2)]
        xr2b = [S(f"xr2b{d}", [128, TS], BF16) for d in range(NDT)]
        sg21 = [S(f"sg21{p}", [128, TS], BF16) for p in range(2)]
        sg22 = [S(f"sg22{p}", [128, TS], BF16) for p in range(2)]
        rcb = [[S(f"rcb{p}_{d}", [128, TS], BF16) for d in range(NDT)]
               for p in range(2)]
        pmu2 = PS("pmu2", [128, TS])
        pe22 = PS("pe22", [128, TS])
        pd2 = [PS(f"pd2{p}", [128, TS]) for p in range(2)]
        pr2 = [PS(f"pr2{p}", [128, TS]) for p in range(2)]
        with nc.Block() as blk:
            @blk.sync
            def _(s):
                for i in range(2 * NDT):
                    if i >= 2:
                        s.wait_ge(s2["cv"], i - 1)
                    nme, dt = ["wo", "cwr"][i // NDT], i % NDT
                    s.dma_start(wstg[i % 2], w_h[nme][dt * 128:(dt + 1) * 128, :]
                                ).then_inc(s2wl[i % 2], 16)
                for st in range(NST):
                    if st >= 2:
                        s.wait_ge(s2["vx"], NDT * (st - 1))
                        s.wait_ge(s2["mmD"], NDT * (st - 1))
                    s.wait_ge(s1st[1], 16 * (4 * st + 4))
                    for dt in range(NDT):
                        s.dma_start(xn2[st % 2][dt], xn_s[st, dt]
                                    ).then_inc(s2ld[st % 2], 16)
                    for dt in range(NDT):
                        s.dma_start(yy2[st % 2][dt], y1_s[st, dt]
                                    ).then_inc(s2ld[st % 2], 16)

            @blk.vector
            def _(v):
                for i in range(2 * NDT):
                    v.wait_ge(s2wl[i % 2], 16 * (i // 2 + 1))
                    dst = [wob, cwrb][i // NDT][i % NDT]
                    v.tensor_copy(dst, wstg[i % 2]).then_inc(s2["cv"], 1)
                for dt in range(NDT):
                    v.tensor_copy(h2buf[dt][:, 0:1], zcol)
                for st in range(NST + 1):
                    if st < NST:
                        for et in range(NDT):
                            ee = st * NDT + et
                            v.wait_ge(s2["mmD"], ee + 1)
                            if st >= 2 and et == 0:
                                v.wait_ge(sstx[st % 2],
                                          16 * NDT * ((st - 2) // 2 + 1))
                            v.tensor_tensor(x2b[st % 2][et], xn2[st % 2][et],
                                            pd2[et % 2], OP.add
                                            ).then_inc(s2["vx"], 1)
                        v.wait_ge(s2["mmS"], 2 * (st + 1))
                        if st >= 2:
                            v.wait_ge(s2["si"], st - 1)
                        v.tensor_copy(musb2[st % 2], pmu2
                                      ).then_inc(s2["vps"], 1)
                        v.wait_ge(s2["vps"], st + 1)
                        v.tensor_tensor(var2[st % 2], musb2[st % 2],
                                        musb2[st % 2], OP.mult)
                        v.scalar_tensor_tensor(var2[st % 2], pe22, 1.0,
                                               var2[st % 2], op0=OP.mult,
                                               op1=OP.subtract
                                               ).then_inc(s2["vv"], 1)
                    if st >= 1:
                        sb = st - 1

                        def _mix2(e):
                            g = sb * NDT + e
                            v.wait_ge(s2["shb"], g + 1)
                            if sb >= 2 and e == 0:
                                v.wait_ge(sstk[sb % 2],
                                          16 * 2 * NDT * ((sb - 2) // 2 + 1))
                            h1 = h2buf[e][:, 0:TS]
                            h_ = h2buf[e][:, 1:TS + 1]
                            v.tensor_tensor(dmix2[:], h_, h1, OP.subtract)
                            v.scalar_tensor_tensor(xk2b[sb % 2][e],
                                                   dmix2[:],
                                                   vc["cmk"][:, e:e + 1], h1,
                                                   op0=OP.mult, op1=OP.add)
                            v.scalar_tensor_tensor(xr2b[e], dmix2[:],
                                                   vc["cmr"][:, e:e + 1],
                                                   h1, op0=OP.mult,
                                                   op1=OP.add
                                                   ).then_inc(s2["vmo"], 1)

                        v.wait_ge(s2["si"], sb + 1)
                        if sb >= 1:
                            v.wait_ge(s2["mmR"], NDT * sb)
                        for dt in range(NDT):
                            g = sb * NDT + dt
                            if g >= 2:
                                v.wait_ge(s2["shb"], g - 1)
                            if sb > 0:
                                v.tensor_copy(h2buf[dt][:, 0:1],
                                              h2buf[dt][:, TS:TS + 1])
                            v.tensor_tensor(t2f[:], x2b[sb % 2][dt],
                                            musb2[sb % 2], OP.subtract)
                            v.tensor_tensor(t2m[dt % 2], t2f[:],
                                            inv2[sb % 2], OP.mult
                                            ).then_inc(s2["vmx"], 1)
                            if dt >= 1:
                                _mix2(dt - 1)
                        _mix2(NDT - 1)

            @blk.scalar
            def _(s):
                for st in range(NST + 1):
                    if st < NST:
                        for dt in range(NDT):
                            s.wait_ge(s2["vx"], st * NDT + dt + 1)
                            if dt == 0 and st >= 1:
                                s.wait_ge(s2["mmS"], 2 * st)
                            s.activation(xsq2[dt], x2b[st % 2][dt], AF.Square
                                         ).then_inc(s2["vq"], 1)
                        s.wait_ge(s2["vv"], st + 1)
                        if st >= 2:
                            s.wait_ge(s2["vmx"], NDT * (st - 1))
                        s.activation(var2[st % 2], var2[st % 2], AF.Ln,
                                     bias=eps3)
                        s.activation(inv2[st % 2], var2[st % 2], AF.Exp,
                                     scale=-0.5).then_inc(s2["si"], 1)
                    if st >= 1:
                        sb = st - 1
                        for dt in range(NDT):
                            g = sb * NDT + dt
                            s.wait_ge(s2["vmx"], g + 1)
                            if sb >= 1:
                                s.wait_ge(s2["vmo"], g - NDT + 1)
                            s.activation(h2buf[dt][:, 1:TS + 1], t2m[dt % 2],
                                         AF.Identity,
                                         scale=vc["ln2g"][:, dt:dt + 1],
                                         bias=vc["ln2b"][:, dt:dt + 1]
                                         ).then_inc(s2["shb"], 1)
                        for ht in range(NDT):
                            hh = sb * NDT + ht
                            p = ht % 2
                            s.wait_ge(s2["mmR"], hh + 1)
                            if sb >= 2 and ht == 0:
                                s.wait_ge(sstk[sb % 2],
                                          16 * 2 * NDT * ((sb - 2) // 2 + 1))
                            s.activation(sg21[p], pr2[p], AF.Exp, scale=-1.0
                                         ).then_inc(s2["sg"], 1)
                            s.activation(sg22[p], sg21[p], AF.Ln,
                                         bias=onecol).then_inc(s2["sg"], 1)
                            s.activation(rcb[sb % 2][ht], sg22[p], AF.Exp,
                                         scale=-1.0).then_inc(s2["sg"], 1)

            @blk.tensor
            def _(t):
                t.wait_ge(s2["cv"], 2 * NDT)
                for st in range(NST + 1):
                    if st < NST:
                        t.wait_ge(s2ld[st % 2], 16 * 2 * NDT * (st // 2 + 1))
                        for et in range(NDT):
                            ee = st * NDT + et
                            if ee >= 2:
                                t.wait_ge(s2["vx"], ee - 1)
                            for dt in range(NDT):
                                ins = t.matmul(
                                    pd2[et % 2],
                                    wob[dt][:, et * 128:(et + 1) * 128],
                                    yy2[st % 2][dt], start=(dt == 0),
                                    stop=(dt == NDT - 1))
                            ins.then_inc(s2["mmD"], 1)
                        t.wait_ge(s2["vx"], NDT * (st + 1))
                        if st > 0:
                            t.wait_ge(s2["vv"], st)
                        for dt in range(NDT):
                            ins = t.matmul(pmu2[:], cst[:], x2b[st % 2][dt],
                                           start=(dt == 0),
                                           stop=(dt == NDT - 1))
                        ins.then_inc(s2["mmS"], 1)
                        for dt in range(NDT):
                            t.wait_ge(s2["vq"], st * NDT + dt + 1)
                            ins = t.matmul(pe22[:], cst[:], xsq2[dt],
                                           start=(dt == 0),
                                           stop=(dt == NDT - 1))
                        ins.then_inc(s2["mmS"], 1)
                    if st >= 1:
                        sb = st - 1
                        t.wait_ge(s2["vmo"], NDT * (sb + 1))
                        for ht in range(NDT):
                            hh = sb * NDT + ht
                            if hh >= 2:
                                t.wait_ge(s2["sg"], 3 * (hh - 2) + 1)
                            for dt in range(NDT):
                                ins = t.matmul(
                                    pr2[ht % 2],
                                    cwrb[dt][:, ht * 128:(ht + 1) * 128],
                                    xr2b[dt], start=(dt == 0),
                                    stop=(dt == NDT - 1))
                            ins.then_inc(s2["mmR"], 1)

            @blk.gpsimd
            def _(gp):
                for st in range(NST + 1):
                    if st < NST:
                        for dt in range(NDT):
                            gp.wait_ge(s2["vx"], st * NDT + dt + 1)
                            gp.dma_start(x2_s[st, dt], x2b[st % 2][dt]
                                         ).then_inc(sstx[st % 2], 16)
                    if st >= 1:
                        sb = st - 1
                        for dt in range(NDT):
                            gp.wait_ge(s2["vmo"], sb * NDT + dt + 1)
                            gp.dma_start(xk2_s[sb, dt], xk2b[sb % 2][dt]
                                         ).then_inc(sstk[sb % 2], 16)
                        for dt in range(NDT):
                            gp.wait_ge(s2["sg"], 3 * (sb * NDT + dt) + 3)
                            gp.dma_start(rc_s[sb, dt], rcb[sb % 2][dt]
                                         ).then_inc(sstk[sb % 2], 16)
                gp.wait_ge(sstx[0], 16 * NDT * ((NST + 1) // 2))
                gp.wait_ge(sstk[0], 16 * 2 * NDT * ((NST + 1) // 2))
                if NST > 1:
                    gp.wait_ge(sstx[1], 16 * NDT * (NST // 2))
                    gp.wait_ge(sstk[1], 16 * 2 * NDT * (NST // 2))

    s3 = {n: SEM("s3" + n) for n in
          ["cv", "mmK", "mmD", "vk", "sq", "va", "tp", "ev", "vps"]}
    s3ld = [SEM("s3lda"), SEM("s3ldb")]
    s3wl = [SEM("s3wla"), SEM("s3wlb")]
    s3st = [SEM("s3sta"), SEM("s3stb")]
    with ExitStack() as es:
        S = lambda nm, sh, dt=F32: es.enter_context(nc.sbuf_tensor(nm, sh, dt))[:]
        PS = lambda nm, sh, dt=F32: es.enter_context(nc.psum_tensor(nm, sh, dt))[:]
        wstg3 = [S(f"w3stg{i}", [128, D]) for i in range(2)]
        cwkb = [S(f"cwkb{i}", [128, FF], BF16) for i in range(NDT)]
        cwvb = [S(f"cwvb{i}", [128, D], BF16) for i in range(NFT)]
        xk3 = [[S(f"xk3{p}_{d}", [128, TS3], BF16) for d in range(NDT)]
               for p in range(2)]
        x23 = [[S(f"x23{p}_{d}", [128, TS3], BF16) for d in range(NDT)]
               for p in range(2)]
        rc3 = [[S(f"rc3{p}_{d}", [128, TS3], BF16) for d in range(NDT)]
               for p in range(2)]
        relu3 = [S(f"relu3{p}", [128, TS3]) for p in range(2)]
        kcb = [S(f"kcb{f}", [128, TS3], BF16) for f in range(NFT)]
        atb = [S(f"atb{p}", [128, TS3], BF16) for p in range(2)]
        atok = [[S(f"atok{p}_{b}", [128, D], BF16) for b in range(NT3)]
                for p in range(2)]
        xx3 = [[S(f"xx3{p}_{b}", [128, D]) for b in range(NT3)]
               for p in range(2)]
        pkc = [PS(f"pkc{p}", [128, TS3]) for p in range(2)]
        pdl = [PS(f"pdl{p}", [128, TS3]) for p in range(2)]
        ptp3 = [PS(f"ptp3{p}", [128, 512], BF16) for p in range(2)]
        def _ev3(v, st, et):
            ee = st * NDT + et
            v.wait_ge(s3["tp"], ee + 1)
            if et == 0 and st >= 2:
                v.wait_ge(s3st[st % 2], 16 * NT3 * ((st - 2) // 2 + 1))
            for tb in range(NT3):
                ins = v.tensor_tensor(
                    atok[st % 2][tb][:, et * 128:(et + 1) * 128],
                    ptp3[et % 2][:, tb * 128:(tb + 1) * 128],
                    xx3[st % 2][tb][:, et * 128:(et + 1) * 128], OP.add)
            ins.then_inc(s3["ev"], 1)

        with nc.Block() as blk:
            @blk.sync
            def _(s):
                for j in range(4 * NDT + NFT):
                    if j >= 2:
                        s.wait_ge(s3["cv"], j - 1)
                    if j < 4 * NDT:
                        dt, c = divmod(j, 4)
                        ap = w_h["cwk"][dt * 128:(dt + 1) * 128,
                                        c * D:(c + 1) * D]
                    else:
                        ft = j - 4 * NDT
                        ap = w_h["cwv"][ft * 128:(ft + 1) * 128, :]
                    s.dma_start(wstg3[j % 2], ap).then_inc(s3wl[j % 2], 16)
                for st in range(NS3):
                    if st >= 2:
                        s.wait_ge(s3["va"], NDT * (st - 1))
                    q, r = divmod(st, 2)
                    for tb in range(NT3):
                        tt = st * NT3 + tb
                        s.dma_start(xx3[st % 2][tb],
                                    x_h[tt * 128:(tt + 1) * 128, :]
                                    ).then_inc(s3ld[st % 2], 16)
                    for dt in range(NDT):
                        s.dma_start(xk3[st % 2][dt],
                                    xk2_s[q, dt][:, r * TS3:(r + 1) * TS3]
                                    ).then_inc(s3ld[st % 2], 16)
                    for dt in range(NDT):
                        s.dma_start(x23[st % 2][dt],
                                    x2_s[q, dt][:, r * TS3:(r + 1) * TS3]
                                    ).then_inc(s3ld[st % 2], 16)
                    for dt in range(NDT):
                        s.dma_start(rc3[st % 2][dt],
                                    rc_s[q, dt][:, r * TS3:(r + 1) * TS3]
                                    ).then_inc(s3ld[st % 2], 16)

            @blk.vector
            def _(v):
                for j in range(4 * NDT + NFT):
                    v.wait_ge(s3wl[j % 2], 16 * (j // 2 + 1))
                    if j < 4 * NDT:
                        dt, c = divmod(j, 4)
                        dst = cwkb[dt][:, c * D:(c + 1) * D]
                    else:
                        dst = cwvb[j - 4 * NDT]
                    v.tensor_copy(dst, wstg3[j % 2]).then_inc(s3["cv"], 1)
                for st in range(NS3):
                    for ft in range(NFT):
                        ff = st * NFT + ft
                        v.wait_ge(s3["mmK"], ff + 1)
                        if ff >= 2:
                            v.wait_ge(s3["sq"], ff - 1)
                        v.tensor_scalar(relu3[ft % 2], pkc[ft % 2], 0.0, None,
                                        OP.max).then_inc(s3["vk"], 1)
                    for et in range(NDT):
                        ee = st * NDT + et
                        v.wait_ge(s3["mmD"], ee + 1)
                        if ee >= 2:
                            v.wait_ge(s3["tp"], ee - 1)
                        v.tensor_tensor(atb[et % 2], rc3[st % 2][et],
                                        pdl[et % 2], OP.mult
                                        ).then_inc(s3["vps"], 1)
                        v.wait_ge(s3["vps"], ee + 1)
                        v.tensor_tensor(atb[et % 2], atb[et % 2],
                                        x23[st % 2][et], OP.add
                                        ).then_inc(s3["va"], 1)
                        if et > 0:
                            _ev3(v, st, et - 1)
                    _ev3(v, st, NDT - 1)

            @blk.scalar
            def _(s):
                for st in range(NS3):
                    for ft in range(NFT):
                        ff = st * NFT + ft
                        s.wait_ge(s3["vk"], ff + 1)
                        if ft == 0 and st >= 1:
                            s.wait_ge(s3["mmD"], NDT * st)
                        s.activation(kcb[ft], relu3[ft % 2], AF.Square
                                     ).then_inc(s3["sq"], 1)

            @blk.tensor
            def _(t):
                t.wait_ge(s3["cv"], 4 * NDT + NFT)
                for st in range(NS3):
                    t.wait_ge(s3ld[st % 2],
                              16 * (3 * NDT + NT3) * (st // 2 + 1))
                    for ft in range(NFT):
                        ff = st * NFT + ft
                        if ff >= 2:
                            t.wait_ge(s3["vk"], ff - 1)
                        for dt in range(NDT):
                            ins = t.matmul(pkc[ft % 2],
                                           cwkb[dt][:, ft * 128:(ft + 1) * 128],
                                           xk3[st % 2][dt], start=(dt == 0),
                                           stop=(dt == NDT - 1))
                        ins.then_inc(s3["mmK"], 1)
                    def _tp3(et):
                        ee = st * NDT + et
                        t.wait_ge(s3["va"], ee + 1)
                        if ee >= 2:
                            t.wait_ge(s3["ev"], ee - 1)
                        for tb in range(NT3):
                            ins = t.transpose(
                                ptp3[et % 2][:, tb * 128:(tb + 1) * 128],
                                atb[et % 2][:, tb * 128:(tb + 1) * 128],
                                idb[:])
                        ins.then_inc(s3["tp"], 1)

                    for et in range(NDT):
                        ee = st * NDT + et
                        if ee >= 2:
                            t.wait_ge(s3["va"], ee - 1)
                        for ft in range(NFT):
                            t.wait_ge(s3["sq"], st * NFT + ft + 1)
                            ins = t.matmul(pdl[et % 2],
                                           cwvb[ft][:, et * 128:(et + 1) * 128],
                                           kcb[ft], start=(ft == 0),
                                           stop=(ft == NFT - 1))
                        ins.then_inc(s3["mmD"], 1)
                        if et > 0:
                            _tp3(et - 1)
                    _tp3(NDT - 1)

            @blk.gpsimd
            def _(gp):
                for st in range(NS3):
                    gp.wait_ge(s3["ev"], NDT * (st + 1))
                    for tb in range(NT3):
                        gp.dma_start(at_s[st * NT3 + tb], atok[st % 2][tb]
                                     ).then_inc(s3st[st % 2], 16)
                gp.wait_ge(s3st[0], 16 * NT3 * ((NS3 + 1) // 2))
                if NS3 > 1:
                    gp.wait_ge(s3st[1], 16 * NT3 * (NS3 // 2))

    es45 = ExitStack()
    w1tT = [es45.enter_context(nc.sbuf_tensor(f"w1tT{i}", [128, FF], BF16))[:]
            for i in range(NDT)]
    w2tT = [es45.enter_context(nc.sbuf_tensor(f"w2tT{i}", [128, D], BF16))[:]
            for i in range(NFT)]

    s4 = {n: SEM("s4" + n) for n in ["ra", "gs", "tn", "tp", "ev", "ms"]}
    s4ld = [SEM("s4lda"), SEM("s4ldb")]
    s4l2 = [SEM("s4l2a"), SEM("s4l2b")]
    mats = [("w1", NFT, D, NDT, w1tT), ("w2", NDT, FF, NFT, w2tT)]
    with ExitStack() as es:
        S = lambda nm, sh, dt=F32: es.enter_context(nc.sbuf_tensor(nm, sh, dt))[:]
        PS = lambda nm, sh, dt=F32: es.enter_context(nc.psum_tensor(nm, sh, dt))[:]
        wstg4 = [S(f"w4stg{i}", [128, FF]) for i in range(2)]
        racc = S("racc", [128, NFT + 2], F32)
        thr = S("thr", [128, 4], F32)
        onesf = S("onesf", [128, 128], F32)
        pthr = PS("pthr", [128, 2])
        s4on = SEM("s4on")
        tb1 = S("tb1", [128, FF], BF16)
        tb2 = S("tb2", [128, FF], BF16)
        wtb = [S(f"wtb{i}", [128, FF], BF16) for i in range(2)]
        ptp4 = [PS(f"ptp4{p}", [128, 512], BF16) for p in range(2)]

        def blocks_before(mi, it):
            b = 0
            for m in range(mi):
                b += mats[m][1] * mats[m][3]
            return b + it * mats[mi][3]

        with nc.Block() as blk:
            @blk.sync
            def _(s):
                s.dma_start(onesf, onesf_h).then_inc(s4on, 16)
                for mi, (nme, ntl, wid, nbk, dst) in enumerate(mats):
                    base = 0 if mi == 0 else mats[0][1]
                    for it in range(ntl):
                        if base + it >= 2:
                            s.wait_ge(s4["ra"],
                                      base + (1 if mi else 0) + it - 1)
                        if mi == 1 and it < 2:
                            s.wait_ge(s4["tn"], mats[0][1])
                        s.dma_start(wstg4[it % 2][:, 0:wid],
                                    w_h[nme][it * 128:(it + 1) * 128, :]
                                    ).then_inc(s4ld[(base + it) % 2], 16)
                    s.wait_ge(s4["gs"], mi + 1)
                    for it in range(ntl):
                        if it >= 2:
                            s.wait_ge(s4["tn"], base + it - 1)
                        s.dma_start(wstg4[it % 2][:, 0:wid],
                                    w_h[nme][it * 128:(it + 1) * 128, :]
                                    ).then_inc(s4l2[(base + it) % 2], 16)

            @blk.vector
            def _(v):
                for mi, (nme, ntl, wid, nbk, dst) in enumerate(mats):
                    base = 0 if mi == 0 else mats[0][1]
                    for it in range(ntl):
                        v.wait_ge(s4ld[(base + it) % 2],
                                  16 * ((base + it) // 2 + 1))
                        v.reduce_sum(racc[:, it:it + 1], wstg4[it % 2][:, 0:wid],
                                     axis=AX, apply_absolute_value=True
                                     ).then_inc(s4["ra"], 1)
                    v.wait_ge(s4["ra"], base + (1 if mi else 0) + ntl)
                    v.reduce_sum(racc[:, NFT + mi:NFT + mi + 1],
                                 racc[:, 0:ntl], axis=AX).then_inc(s4["ra"], 1)
                    v.wait_ge(s4["ms"], mi + 1)
                    v.tensor_scalar(thr[:, mi:mi + 1], pthr[:, mi:mi + 1],
                                    0.5 / (FF * D), None, OP.mult)
                    v.tensor_scalar(thr[:, 2 + mi:3 + mi], pthr[:, mi:mi + 1],
                                    -0.5 / (FF * D), None, OP.mult
                                    ).then_inc(s4["gs"], 1)
                    for it in range(ntl):
                        v.wait_ge(s4l2[(base + it) % 2],
                                  16 * ((base + it) // 2 + 1))
                        if it >= 2:
                            v.wait_ge(s4["tp"], blocks_before(mi, it - 1))
                        elif mi == 1:
                            v.wait_ge(s4["tp"], blocks_before(0, mats[0][1]))
                        v.tensor_single_scalar(tb1[:, 0:wid],
                                               wstg4[it % 2][:, 0:wid],
                                               thr[:, mi:mi + 1], OP.is_gt)
                        v.tensor_single_scalar(tb2[:, 0:wid],
                                               wstg4[it % 2][:, 0:wid],
                                               thr[:, 2 + mi:3 + mi], OP.is_lt)
                        v.tensor_tensor(wtb[it % 2][:, 0:wid], tb1[:, 0:wid],
                                        tb2[:, 0:wid], OP.subtract
                                        ).then_inc(s4["tn"], 1)

            @blk.tensor
            def _(t):
                t.wait_ge(s4on, 16)
                nblk = 0
                for mi, (nme, ntl, wid, nbk, dst) in enumerate(mats):
                    base = 0 if mi == 0 else mats[0][1]
                    need = (mats[0][1] + 1) if mi == 0 else \
                        (mats[0][1] + mats[1][1] + 2)
                    t.wait_ge(s4["ra"], need)
                    t.matmul(pthr[:, mi:mi + 1], onesf,
                             racc[:, NFT + mi:NFT + mi + 1],
                             start=True, stop=True).then_inc(s4["ms"], 1)
                    for it in range(ntl):
                        t.wait_ge(s4["tn"], base + it + 1)
                        for c in range(nbk):
                            if nblk % 4 == 0 and nblk >= 8:
                                t.wait_ge(s4["ev"], nblk - 4)
                            ins = t.transpose(
                                ptp4[(nblk // 4) % 2]
                                [:, (nblk % 4) * 128:(nblk % 4 + 1) * 128],
                                wtb[it % 2][:, c * 128:(c + 1) * 128], idb[:])
                            if nblk % 4 == 3:
                                ins.then_inc(s4["tp"], 4)
                            nblk += 1

            @blk.scalar
            def _(s):
                nblk = 0
                for mi, (nme, ntl, wid, nbk, dst) in enumerate(mats):
                    for it in range(ntl):
                        for c in range(nbk):
                            if nblk % 4 == 0:
                                s.wait_ge(s4["tp"], nblk + 4)
                            ins = s.copy(dst[c][:, it * 128:(it + 1) * 128],
                                         ptp4[(nblk // 4) % 2]
                                         [:, (nblk % 4) * 128:
                                          (nblk % 4 + 1) * 128])
                            if nblk % 4 == 3:
                                ins.then_inc(s4["ev"], 4)
                            nblk += 1

    TS5 = 512
    NS5 = T // TS5
    NT5 = TS5 // 128
    s5 = {n: SEM("s5" + n) for n in ["mmF", "mmO", "sg", "so"]}
    s5ld = [SEM("s5lda"), SEM("s5ldb")]
    s5st = [SEM("s5sta"), SEM("s5stb")]
    with ExitStack() as es:
        S = lambda nm, sh, dt=F32: es.enter_context(nc.sbuf_tensor(nm, sh, dt))[:]
        PS = lambda nm, sh, dt=F32: es.enter_context(nc.psum_tensor(nm, sh, dt))[:]
        xn5 = [[S(f"xn5{p}_{d}", [128, TS5], BF16) for d in range(NDT)]
               for p in range(2)]
        at5 = [[S(f"at5{p}_{b}", [128, D], BF16) for b in range(NT5)]
               for p in range(2)]
        glb = [S(f"glb{f}", [128, TS5], BF16) for f in range(NFT)]
        outb = [S(f"outb{b}", [128, 512]) for b in range(4)]
        pff = [PS(f"pff{p}", [128, TS5]) for p in range(2)]
        pout = [PS(f"pout{p}", [128, 512]) for p in range(2)]
        with nc.Block() as blk:
            @blk.sync
            def _(s):
                for st in range(NS5):
                    if st >= 2:
                        s.wait_ge(s5["so"], 2 * NT5 * (st - 1))
                    for dt in range(NDT):
                        s.dma_start(xn5[st % 2][dt], xn_s[st, dt]
                                    ).then_inc(s5ld[st % 2], 16)
                    for tb in range(NT5):
                        tt = st * NT5 + tb
                        s.dma_start(at5[st % 2][tb], at_s[tt]
                                    ).then_inc(s5ld[st % 2], 16)

            @blk.tensor
            def _(t):
                for st in range(NS5):
                    t.wait_ge(s5ld[st % 2], 16 * (NDT + NT5) * (st // 2 + 1))
                    for ft in range(NFT):
                        ff = st * NFT + ft
                        if ff >= 2:
                            t.wait_ge(s5["sg"], ff - 1)
                        for dt in range(NDT):
                            ins = t.matmul(pff[ft % 2],
                                           w1tT[dt][:, ft * 128:(ft + 1) * 128],
                                           xn5[st % 2][dt], start=(dt == 0),
                                           stop=(dt == NDT - 1))
                        ins.then_inc(s5["mmF"], 1)
                    t.wait_ge(s5["sg"], (st + 1) * NFT)
                    for tb in range(NT5):
                        oo = st * NT5 * 2 + tb * 2
                        if oo >= 2:
                            t.wait_ge(s5["so"], oo)
                        for ft in range(NFT):
                            for ec in range(2):
                                t.matmul(pout[ec],
                                         glb[ft][:, tb * 128:(tb + 1) * 128],
                                         w2tT[ft][:, ec * 512:(ec + 1) * 512],
                                         start=(ft == 0), stop=False,
                                         skip_group_check=True)
                        for ec in range(2):
                            t.matmul(pout[ec], onr[:],
                                     b2r[:, ec * 512:(ec + 1) * 512],
                                     start=False, stop=False,
                                     skip_group_check=True)
                            ins = t.matmul(
                                pout[ec], idb[:],
                                at5[st % 2][tb][:, ec * 512:(ec + 1) * 512],
                                start=False, stop=True,
                                skip_group_check=True)
                        ins.then_inc(s5["mmO"], 2)

            @blk.scalar
            def _(s):
                for st in range(NS5):
                    for ft in range(NFT):
                        ff = st * NFT + ft
                        s.wait_ge(s5["mmF"], ff + 1)
                        if ft == 0 and st >= 1:
                            s.wait_ge(s5["mmO"], 2 * NT5 * st)
                        s.activation(glb[ft], pff[ft % 2], AF.Gelu,
                                     bias=b1c[:, ft:ft + 1]).then_inc(s5["sg"], 1)
                    for tb in range(NT5):
                        for ec in range(2):
                            oo = st * NT5 * 2 + tb * 2 + ec
                            j = tb * 2 + ec
                            s.wait_ge(s5["mmO"], st * NT5 * 2 + tb * 2 + 2)
                            if st >= 1 and j == 0:
                                s.wait_ge(s5st[(st - 1) % 2],
                                          16 * 2 * NT5 * ((st - 1) // 2 + 1))
                            if j >= 4:
                                s.wait_ge(s5st[st % 2],
                                          16 * (2 * NT5 * (st // 2) + j - 3))
                            s.copy(outb[j % 4], pout[ec]
                                   ).then_inc(s5["so"], 1)

            @blk.gpsimd
            def _(gp):
                for st in range(NS5):
                    for tb in range(NT5):
                        for ec in range(2):
                            oo = st * NT5 * 2 + tb * 2 + ec
                            tt = st * NT5 + tb
                            gp.wait_ge(s5["so"], oo + 1)
                            gp.dma_start(
                                y_h[tt * 128:(tt + 1) * 128,
                                    ec * 512:(ec + 1) * 512],
                                outb[(tb * 2 + ec) % 4]
                                ).then_inc(s5st[st % 2], 16)
                gp.wait_ge(s5st[0], 16 * 2 * NT5 * ((NS5 + 1) // 2))
                if NS5 > 1:
                    gp.wait_ge(s5st[1], 16 * 2 * NT5 * (NS5 // 2))

    es45.close()
    return nc


def make_inputs_per_core(inp, xc):
    g = lambda n: np.ascontiguousarray(np.asarray(inp["L0_attn_" + n],
                                                  dtype=np.float32))
    m = {
        "x": np.ascontiguousarray(np.asarray(xc, dtype=np.float32)),
        "wk": g("tm_Wk"), "wv": g("tm_Wv"), "wr": g("tm_Wr"), "wo": g("Wo"),
        "cwr": g("cm_Wr"), "cwk": g("cm_Wk"), "cwv": g("cm_Wv"),
        "w1": np.ascontiguousarray(np.asarray(inp["L0_ffn_W1"], np.float32)),
        "w2": np.ascontiguousarray(np.asarray(inp["L0_ffn_W2"], np.float32)),
        "ln1g": g("ln1_g"), "ln1b": g("ln1_b"), "ln2g": g("ln2_g"),
        "ln2b": g("ln2_b"), "mk": g("tm_mix_k"), "mv": g("tm_mix_v"),
        "mr": g("tm_mix_r"), "td": g("time_decay"), "tf": g("time_first"),
        "cmk": g("cm_mix_k"), "cmr": g("cm_mix_r"),
        "b1": np.ascontiguousarray(np.asarray(inp["L0_ffn_b1"], np.float32)),
        "b2": np.ascontiguousarray(np.asarray(inp["L0_ffn_b2"], np.float32)),
        "idf_h": np.eye(128, dtype=np.float32),
        "cc_h": np.array([[1.0, 1e-3, 1e-5, 0.0]] * 128, dtype=np.float32),
        "onesf_h": np.ones((128, 128), dtype=np.float32),
        "onesb_h": np.ones((128, 512), dtype=bf16),
        "idb_h": np.eye(128, dtype=bf16),
        "cst_h": np.full((128, 128), 1.0 / 1024, dtype=bf16),
        "onr_h": np.ones((1, 128), dtype=bf16),
    }
    return m


def kernel(**inputs):
    x = np.asarray(inputs["L0_attn_inputs"])
    nc = build(T=x.shape[1])
    in_maps = [make_inputs_per_core(inputs, x[c]) for c in range(8)]
    res = run_bass_kernel_spmd(nc, in_maps, core_ids=list(range(8)))
    return np.stack([np.asarray(res.results[c]["y"]) for c in range(8)], axis=0)
